# version 20
# baseline (speedup 1.0000x reference)
import numpy as np
import ml_dtypes
import concourse.bass as bass
import concourse.mybir as mybir
from concourse.bass import IndirectOffsetOnAxis
from concourse.bass_utils import run_bass_kernel_spmd
from contextlib import ExitStack

F32 = mybir.dt.float32
BF16 = mybir.dt.bfloat16
AF = mybir.ActivationFunctionType
AX = mybir.AxisListType
OP = mybir.AluOpType

NCORES = 8
SPC = 2
T = 2048
D = 1024
NC = 8
NTT = 16
NQG = 4
HA = 16
HF = 16
KVH = 4
DH = 64
NE = 32
DFF = 1024

HEAD_ORDER = [0, 4, 1, 5, 2, 6, 3, 7, 8, 12, 9, 13, 10, 14, 11, 15]

DEBUG = False

ENGINES = ("sync", "tensor", "vector", "scalar", "gpsimd")


class Sem:
    __slots__ = ("h", "n")

    def __init__(self, h):
        self.h = h
        self.n = 0


class Plan:
    def __init__(self):
        self.q = {e: [] for e in ENGINES}

    def op(self, eng, fn, incs=()):
        def run(e, fn=fn, incs=tuple(incs)):
            ins = fn(e)
            for s, k in incs:
                ins.then_inc(s.h, k)
        self.q[eng].append(run)
        last = 0
        for s, k in incs:
            s.n += k
            last = s.n
        return last

    def wait(self, eng, sem, n=None):
        v = sem.n if n is None else n
        if v is None or v <= 0:
            return
        self.q[eng].append(lambda e, h=sem.h, v=v: e.wait_ge(h, v))


def build_program():
    nc = bass.Bass("TRN2", target_bir_lowering=False)

    x_d = nc.dram_tensor("x", [SPC, T, D], F32, kind="ExternalInput").ap()
    wqkva_d = nc.dram_tensor("wqkva", [D, 3 * D], F32, kind="ExternalInput").ap()
    wouta_d = nc.dram_tensor("wouta", [D, D], F32, kind="ExternalInput").ap()
    anw_d = nc.dram_tensor("anw", [D], F32, kind="ExternalInput").ap()
    fnw_d = nc.dram_tensor("fnw", [D], F32, kind="ExternalInput").ap()
    qkvwf_d = nc.dram_tensor("qkvwf", [D, 1536], F32, kind="ExternalInput").ap()
    qkvbf_d = nc.dram_tensor("qkvbf", [1, 1536], F32, kind="ExternalInput").ap()
    projw_d = nc.dram_tensor("projw", [D, D], F32, kind="ExternalInput").ap()
    projb_d = nc.dram_tensor("projb", [1, D], F32, kind="ExternalInput").ap()
    sink_d = nc.dram_tensor("sink", [1, HF], F32, kind="ExternalInput").ap()
    gatew_d = nc.dram_tensor("gatew", [D, NE], F32, kind="ExternalInput").ap()
    gateb_d = nc.dram_tensor("gateb", [1, NE], F32, kind="ExternalInput").ap()
    wup_d = nc.dram_tensor("wup", [NE, D, 2 * DFF], F32, kind="ExternalInput").ap()
    wdn_d = nc.dram_tensor("wdn", [NE, DFF, D], F32, kind="ExternalInput").ap()
    ident_d = nc.dram_tensor("ident", [128, 128], BF16, kind="ExternalInput").ap()
    cosq_d = nc.dram_tensor("cosq", [128, T], BF16, kind="ExternalInput").ap()
    sinq_d = nc.dram_tensor("sinq", [128, T], BF16, kind="ExternalInput").ap()
    maskf_d = nc.dram_tensor("maskf", [5, 128, 512], BF16, kind="ExternalInput").ap()
    lt16_d = nc.dram_tensor("lt16", [16, 16], BF16, kind="ExternalInput").ap()
    lt128_d = nc.dram_tensor("lt128", [128, 128], BF16, kind="ExternalInput").ap()
    eoff_d = nc.dram_tensor("eoff", [1, 64], BF16, kind="ExternalInput").ap()
    gbuf_d = nc.dram_tensor("gbuf", [NE * 1024, D], BF16).ap()
    dnbuf_d = nc.dram_tensor("dnbuf", [NE * 1024, D], BF16).ap()
    cnt_d = nc.dram_tensor("cnt_d", [NE, NTT], F32).ap()
    base_d = nc.dram_tensor("base_d", [NTT, NE], F32).ap()
    out_d = nc.dram_tensor("out", [SPC, T, D], F32, kind="ExternalOutput").ap()
    if DEBUG:
        dbg_stats = nc.dram_tensor("dbg_stats", [128, NTT, 4], F32, kind="ExternalOutput").ap()
        dbg_xn = nc.dram_tensor("dbg_xn", [128, NC, T], F32, kind="ExternalOutput").ap()
        dbg_ka = nc.dram_tensor("dbg_ka", [128, NC, T], F32, kind="ExternalOutput").ap()
        dbg_qa = nc.dram_tensor("dbg_qa", [128, NC, 512], F32, kind="ExternalOutput").ap()
        dbg_va = nc.dram_tensor("dbg_va", [128, NTT, HA * 65], F32, kind="ExternalOutput").ap()
        dbg_p = nc.dram_tensor("dbg_p", [128, 2, 1024], F32, kind="ExternalOutput").ap()
        dbg_atno = nc.dram_tensor("dbg_atno", [128, NC, 512], F32, kind="ExternalOutput").ap()
        dbg_hb = nc.dram_tensor("dbg_hb", [128, NC, T], F32, kind="ExternalOutput").ap()
        dbg_kf = nc.dram_tensor("dbg_kf", [128, 2, T], F32, kind="ExternalOutput").ap()
        dbg_qf = nc.dram_tensor("dbg_qf", [128, NC, 512], F32, kind="ExternalOutput").ap()
        dbg_yf = nc.dram_tensor("dbg_yf", [128, NC, 512], F32, kind="ExternalOutput").ap()
        dbg_xf = nc.dram_tensor("dbg_xf", [128, NC, T], F32, kind="ExternalOutput").ap()
        dbg_rstdh2 = nc.dram_tensor("dbg_rstdh2", [1, T], F32, kind="ExternalOutput").ap()
        dbg_h2 = nc.dram_tensor("dbg_h2", [128, NC, T], F32, kind="ExternalOutput").ap()
        dbg_sel = nc.dram_tensor("dbg_sel", [128, SPC, NTT, 4], F32, kind="ExternalOutput").ap()
        dbg_gw = nc.dram_tensor("dbg_gw", [128, SPC, NTT, 4], F32, kind="ExternalOutput").ap()
        dbg_base = nc.dram_tensor("dbg_base", [1, 512], F32, kind="ExternalOutput").ap()
        dbg_gth = nc.dram_tensor("dbg_gth", [128, 4, 1024], F32, kind="ExternalOutput").ap()
        dbg_gbuf = nc.dram_tensor("dbg_gbuf", [256, 1024], F32, kind="ExternalOutput").ap()
        dbg_geq = nc.dram_tensor("dbg_geq", [128, NTT, 4, NE], F32, kind="ExternalOutput").ap()
        dbg_bf = nc.dram_tensor("dbg_bf", [128, 512], F32, kind="ExternalOutput").ap()
        dbg_sfl = nc.dram_tensor("dbg_sfl", [128, NTT, 4], F32, kind="ExternalOutput").ap()


    xr = x_d.rearrange("s (t p) f -> p s t f", p=128)
    outr = out_d.rearrange("s (t p) f -> p s t f", p=128)
    wqkva_r = wqkva_d.rearrange("(c p) n -> p c n", p=128)
    wouta_r = wouta_d.rearrange("(c p) n -> p c n", p=128)
    qkvwf_r = qkvwf_d.rearrange("(c p) n -> p c n", p=128)
    projw_r = projw_d.rearrange("(c p) n -> p c n", p=128)
    gatew_r = gatew_d.rearrange("(c p) n -> p c n", p=128)
    anw_r = anw_d.rearrange("(c p) -> p c", p=128)
    fnw_r = fnw_d.rearrange("(c p) -> p c", p=128)
    wup_r = wup_d.rearrange("e (c p) n -> p e c n", p=128)
    wdn_r = wdn_d.rearrange("e (c p) n -> p e c n", p=128)

    P = Plan()
    op, wait = P.op, P.wait
    es = ExitStack()
    with es:
        def sbuf(name, shape, dt):
            return es.enter_context(nc.sbuf_tensor(name, shape, dt))

        ident = sbuf("ident_s", [128, 128], BF16)
        xn = sbuf("xn_s", [128, NC, T], BF16)
        stats = sbuf("stats_s", [128, NTT, 4], F32)
        onesb = sbuf("onesb", [1, 512], BF16)
        onescol = sbuf("onescol", [128, 1], BF16)
        onesf = sbuf("onesf", [1, 128], F32)
        anw = sbuf("anw_s", [128, NC], F32)
        fnw = sbuf("fnw_s", [128, NC], F32)
        gatew = sbuf("gatew_s", [128, NC, NE], BF16)
        gateb = sbuf("gateb_s", [1, NE], BF16)
        sink = sbuf("sink_s", [1, HF], F32)
        sinkexp = sbuf("sinkexp", [1, HF], F32)
        rstdh2 = sbuf("rstdh2", [1, T], BF16)
        hrow = sbuf("hrow", [1, 512], BF16)
        hrow2 = sbuf("hrow2", [1, 2, 512], BF16)
        rstrow = sbuf("rstrow", [1, 2, 512], F32)
        rowtmp = sbuf("rowtmp", [1, 2, 512], F32)
        epsb = sbuf("epsb", [128, 1], F32)
        lt16b = sbuf("lt16b", [16, 16], BF16)
        lt128b = sbuf("lt128b", [128, 128], BF16)
        eoffb = sbuf("eoffb", [1, 64], BF16)
        selint = sbuf("selint", [128, SPC, NTT, 4], mybir.dt.int32)
        gwsave = sbuf("gwsave", [128, SPC, NTT, 4], F32)

        AKB = 140
        arena = sbuf("arena", [128, AKB * 1024 // 4], F32)
        a_bf = arena.bitcast(BF16).ap()
        a_f32 = arena.ap()

        def carve(bf, off_kb, shape):
            base = a_bf if bf else a_f32
            isz = 2 if bf else 4
            start = off_kb * 1024 // isz
            n = int(np.prod(shape))
            ap = base[:, start:start + n]
            if len(shape) == 1:
                return ap
            pat = {2: "p (a b) -> p a b", 3: "p (a b c) -> p a b c"}[len(shape)]
            return ap.rearrange(pat, **dict(zip("abc", shape)))

        xbuf = carve(False, 32, [2, 1024])
        xntm = carve(True, 72, [1024])
        sqbuf = carve(False, 74, [2, 1024])
        wbuf2 = carve(True, 0, [NC, 2048])
        Qa = carve(True, 32, [NC, 512])
        Ka = carve(True, 40, [NC, T])
        Va = carve(True, 72, [NTT, HA * 65])
        atno = carve(True, 106, [2, NC, 512])
        Pb2 = carve(True, 122, [4, 1024])
        tmp2 = carve(False, 130, [4, 512])
        wqkvf = carve(True, 0, [NC, 1024])
        wproj = carve(True, 16, [NC, 1024])
        hb = carve(True, 32, [NC, T])
        Qf = carve(True, 64, [NC, 512])
        Kf = carve(True, 72, [2, T])
        Vf = carve(True, 80, [NTT, KVH * 65])
        yf = carve(True, 89, [NC, 512])
        xnsq = carve(True, 97, [2, 512])
        swp = carve(True, 99, [2, 1024])
        cosb = carve(True, 103, [T])
        sinb = carve(True, 107, [T])
        maskb = carve(True, 111, [5, 512])
        xbuf3 = carve(False, 116, [1024])
        qkvbrow = carve(True, 120, [1536])
        projbrow = carve(True, 123, [1024])
        Pb3 = carve(True, 125, [4, 512])
        tmp3 = carve(False, 129, [2, 1024])
        h2 = carve(True, 0, [NC, T])
        wu = carve(True, 32, [2, NC, 1024])
        wd = carve(True, 64, [NC, 1024])
        G_fm2 = carve(True, 80, [NC, 1024])
        gwork = carve(False, 96, [NTT, NE])
        geq = carve(False, 98, [NTT, 4, NE])
        gm = carve(False, 106, [NTT, 8])
        indb = carve(True, 107, [NTT, NE])
        sfl = carve(False, 108, [NTT, 4])
        sfield = carve(False, 109, [NE])
        cntsb = carve(False, 110, [16])
        cntC = carve(True, 110, [64])
        basesb = carve(False, 111, [NE])
        baserow = carve(False, 112, [512])
        bfield = carve(False, 114, [512])
        h2tm = carve(True, 116, [2, 1024])
        Gtm = carve(True, 0, [2, NC, 1024])
        G_fm1 = carve(True, 96, [NC, 1024])
        act = carve(True, 112, [NC, 1024])
        tmpbf = carve(True, 128, [2, 512])
        gth = carve(True, 96, [2, 4, 1024])
        tmpgs = carve(False, 112, [2, 1024])
        tmpo = carve(False, 0, [2, 1024])
        xfsq = carve(True, 138, [2, 512])

        psA = [es.enter_context(nc.psum_tensor(f"psA{i}", [128, 1024], F32)) for i in range(2)]
        psB = [es.enter_context(nc.psum_tensor(f"psB{i}", [128, 512], F32)) for i in range(2)]
        psC = es.enter_context(nc.psum_tensor("psC", [128, 512], F32))
        psT = es.enter_context(nc.psum_tensor("psT", [128, 1024], BF16))

        dX = Sem(es.enter_context(nc.semaphore("dX")))
        dW = Sem(es.enter_context(nc.semaphore("dW")))
        dO = Sem(es.enter_context(nc.semaphore("dO")))
        tM = Sem(es.enter_context(nc.semaphore("tM")))
        vM = Sem(es.enter_context(nc.semaphore("vM")))
        sM = Sem(es.enter_context(nc.semaphore("sM")))
        tP = Sem(es.enter_context(nc.semaphore("tP")))
        dC = Sem(es.enter_context(nc.semaphore("dC")))
        dXp = [Sem(es.enter_context(nc.semaphore(f"dXp{i}"))) for i in range(2)]
        dX3 = Sem(es.enter_context(nc.semaphore("dX3")))
        dWug = Sem(es.enter_context(nc.semaphore("dWug")))
        dWuu = Sem(es.enter_context(nc.semaphore("dWuu")))
        dWdd = Sem(es.enter_context(nc.semaphore("dWdd")))
        dWgt = Sem(es.enter_context(nc.semaphore("dWgt")))
        dSC = [Sem(es.enter_context(nc.semaphore(f"dSC{i}"))) for i in range(2)]
        dDN = [Sem(es.enter_context(nc.semaphore(f"dDN{i}"))) for i in range(2)]
        dGT = [Sem(es.enter_context(nc.semaphore(f"dGT{i}"))) for i in range(2)]
        dOA = [Sem(es.enter_context(nc.semaphore(f"dOA{i}"))) for i in range(2)]
        dOX = [Sem(es.enter_context(nc.semaphore(f"dOX{i}"))) for i in range(2)]
        dOG = [Sem(es.enter_context(nc.semaphore(f"dOG{i}"))) for i in range(2)]
        dOL = [Sem(es.enter_context(nc.semaphore(f"dOL{i}"))) for i in range(2)]
        psc_free = [0]

        def mm(ps_ap, lhsT, rhs, st, sp, incs=(), tile_position=None):
            return op("tensor",
                      lambda e, a=ps_ap, l=lhsT, r=rhs, st=st, sp=sp, tp_=tile_position:
                          e.matmul(a, l, r, start=st, stop=sp, tile_position=tp_), incs)

        def mm_group(ps_ap, pairs, incs=()):
            n = len(pairs)
            last = 0
            for i, (lf, rf) in enumerate(pairs):
                last = mm(ps_ap, lf, rf, i == 0, i == n - 1,
                          incs if i == n - 1 else ())
            return last

        def act3(fp, qg):
            return act[:, 2 * fp:2 * fp + 2, qg * 512:(qg + 1) * 512]

        c_w2kv = op("gpsimd", lambda e:
                    e.dma_start(out=wbuf2[:, :, :], in_=wqkva_r[:, :, 1024:3072]), [(dW, 16)])
        c_ident = op("sync", lambda e: e.dma_start(out=ident[:, :], in_=ident_d), [(dC, 16)])
        c_sink = op("sync", lambda e: e.dma_start(out=sink[0:1, :], in_=sink_d), [(dC, 16)])
        op("gpsimd", lambda e: e.dma_start(out=anw[:, :], in_=anw_r, allow_slow_non_contiguous=True), [(dW, 16)])
        op("gpsimd", lambda e: e.dma_start(out=fnw[:, :], in_=fnw_r, allow_slow_non_contiguous=True), [(dW, 16)])
        op("gpsimd", lambda e: e.dma_start(out=gatew[:, :, :], in_=gatew_r), [(dW, 16)])
        op("gpsimd", lambda e: e.dma_start(out=gateb[0:1, :], in_=gateb_d), [(dW, 16)])
        op("vector", lambda e: e.memset(onesb[0:1, :], 1.0), [(vM, 1)])
        op("vector", lambda e: e.memset(onescol[:, :], 1.0), [(vM, 1)])
        c_ones = op("vector", lambda e: e.memset(onesf[0:1, :], 1.0), [(vM, 1)])
        c_eps = op("vector", lambda e: e.memset(epsb[:, :], 1e-5), [(vM, 1)])
        op("sync", lambda e: e.dma_start(out=lt16b[:, :], in_=lt16_d), [(dC, 16)])
        op("sync", lambda e: e.dma_start(out=lt128b[:, :], in_=lt128_d), [(dC, 16)])
        c_lt = op("sync", lambda e: e.dma_start(out=eoffb[0:1, :], in_=eoff_d), [(dC, 16)])
        wait("scalar", dC, dC.n)
        op("scalar", lambda e: e.activation(sinkexp[0:1, :], sink[0:1, :], AF.Exp), [(sM, 1)])

        def psa_slot(j):
            return psA[(j // 2) % 2][:, (j % 2) * 512:(j % 2 + 1) * 512]

        seq_scatter_done = {}
        for s in range(SPC):
            wait("sync", tM, tM.n)
            wait("sync", vM, vM.n)
            wait("sync", sM, sM.n)
            wait("sync", dX, dX.n)
            wait("sync", dC, dC.n)
            wait("sync", dXp[0], dXp[0].n)
            wait("sync", dXp[1], dXp[1].n)
            wait("sync", dX3, dX3.n)
            if s >= 1:
                wait("gpsimd", tM, tM.n)
                wait("gpsimd", vM, vM.n)
                wait("gpsimd", sM, sM.n)
                c_w2kv = op("gpsimd", lambda e:
                            e.dma_start(out=wbuf2[:, :, :], in_=wqkva_r[:, :, 1024:3072]), [(dW, 16)])
            ld = [0] * NTT
            p1v = [0] * NTT
            p1t = [0] * NTT
            p1e = [0] * NTT
            sq = [0] * NTT
            r1 = [0] * NTT
            evacA = {}
            jk = 0
            for t in range(NTT):
                b = t % 2
                if t >= 2:
                    wait("sync", vM, p1v[t - 2])
                    wait("sync", dXp[b], dXp[b].n)
                ld[t] = op("sync", lambda e, s=s, t=t, b=b:
                           e.dma_start(out=xbuf[:, b, :], in_=xr[:, s, t, :]), [(dXp[b], 16)])
                wait("scalar", dXp[b], ld[t])
                if t >= 2:
                    wait("scalar", vM, r1[t - 2])
                sq[t] = op("scalar", lambda e, b=b:
                           e.activation(sqbuf[:, b, :], xbuf[:, b, :], AF.Square), [(sM, 1)])
                wait("vector", dXp[b], ld[t])
                op("vector", lambda e, b=b, t=t:
                   e.reduce_sum(stats[:, t, 0:1], xbuf[:, b, :], axis=AX.X), [(vM, 1)])
                wait("vector", sM, sq[t])
                r1[t] = op("vector", lambda e, b=b, t=t:
                           e.reduce_sum(stats[:, t, 1:2], sqbuf[:, b, :], axis=AX.X), [(vM, 1)])
                a1 = op("vector", lambda e, t=t:
                        e.tensor_scalar_mul(stats[:, t, 2:3], stats[:, t, 0:1], -1.0 / D), [(vM, 1)])
                wait("vector", vM, a1)
                a2 = op("vector", lambda e, t=t:
                        e.tensor_tensor(stats[:, t, 3:4], stats[:, t, 2:3], stats[:, t, 2:3], OP.mult), [(vM, 1)])
                a3 = op("vector", lambda e, t=t:
                        e.tensor_scalar_mul(stats[:, t, 1:2], stats[:, t, 1:2], 1.0 / D), [(vM, 1)])
                wait("vector", vM, a3)
                a4 = op("vector", lambda e, t=t:
                        e.scalar_tensor_tensor(stats[:, t, 1:2], stats[:, t, 3:4], -1.0,
                                               stats[:, t, 1:2], OP.mult, OP.add), [(vM, 1)])
                wait("scalar", vM, a4)
                b1 = op("scalar", lambda e, t=t:
                        e.activation(stats[:, t, 3:4], stats[:, t, 1:2], AF.Ln, bias=epsb[:, 0:1]), [(sM, 1)])
                wait("scalar", sM, b1)
                b2 = op("scalar", lambda e, t=t:
                        e.activation(stats[:, t, 3:4], stats[:, t, 3:4], AF.Exp, scale=-0.5), [(sM, 1)])
                wait("vector", sM, b2)
                if t >= 1:
                    wait("vector", tM, p1t[t - 1])
                p1v[t] = op("vector", lambda e, b=b, t=t:
                            e.tensor_scalar(xntm[:, :], xbuf[:, b, :], stats[:, t, 2:3],
                                            stats[:, t, 3:4], OP.add, OP.mult), [(vM, 1)])
                wait("tensor", vM, p1v[t])
                if t == 0 and s == 0:
                    wait("tensor", dC, c_ident)
                if t >= 1:
                    wait("tensor", vM, p1e[t - 1])
                for c in range(NC):
                    p1t[t] = op("tensor", lambda e, c=c:
                                e.transpose(psT[:, c * 128:(c + 1) * 128],
                                            xntm[:, c * 128:(c + 1) * 128], ident[:, :]),
                                [(tM, 1)] if c == NC - 1 else ())
                wait("vector", tM, p1t[t])
                p1e[t] = op("vector", lambda e, t=t:
                            e.tensor_copy(xn[:, :, t * 128:(t + 1) * 128],
                                          psT[:, :].rearrange("p (a b) -> p a b", a=NC, b=128)),
                            [(vM, 1)])
                if t % 4 == 3:
                    qgk = t // 4
                    for kc in range(NC):
                        sl = psa_slot(jk)
                        wait("tensor", dW, c_w2kv)
                        wait("tensor", vM, p1e[t])
                        if jk >= 4:
                            wait("tensor", vM, evacA[jk - 4])
                        g = mm_group(sl, [(wbuf2[:, c, kc * 128:(kc + 1) * 128],
                                           xn[:, c, qgk * 512:(qgk + 1) * 512])
                                          for c in range(NC)], [(tM, 1)])
                        wait("vector", tM, g)
                        evacA[jk] = op("vector", lambda e, sl=sl, kc=kc, qgk=qgk:
                                       e.tensor_copy(Ka[:, kc, qgk * 512:(qgk + 1) * 512], sl),
                                       [(vM, 1)])
                        jk += 1

            if DEBUG and s == 0:
                wait("sync", vM, vM.n)
                op("sync", lambda e: e.dma_start(out=dbg_stats, in_=stats[:, :, :]), [(dX, 16)])
                wait("gpsimd", vM, vM.n)
                op("gpsimd", lambda e: e.dma_start(out=dbg_xn, in_=xn[:, :, :]), [(dX, 16)])

            op("vector", lambda e: e.memset(
                Va[:, :, :].rearrange("p t (h x) -> p t h x", x=65)[:, :, :, 64:65], 1.0), [(vM, 1)])
            jb = 0
            evacB = {}
            for tt in range(NTT):
                for vb in range(2):
                    pb = psB[jb % 2]
                    if jb >= 2:
                        wait("tensor", vM, evacB[jb - 2])
                    g = mm_group(pb[:, :], [(xn[:, c, tt * 128:(tt + 1) * 128],
                                             wbuf2[:, c, 1024 + vb * 512:1024 + (vb + 1) * 512])
                                            for c in range(NC)], [(tM, 1)])
                    wait("vector", tM, g)
                    dst = Va[:, tt, vb * 8 * 65:(vb * 8 + 8) * 65].rearrange(
                        "p (a b) -> p a b", a=8, b=65)[:, :, 0:64]
                    evacB[jb] = op("vector", lambda e, pb=pb, dst=dst:
                                   e.tensor_copy(dst, pb[:, :].rearrange("p (a b) -> p a b", a=8, b=64)),
                                   [(vM, 1)])
                    jb += 1
            wait("gpsimd", tM, tM.n)
            c_w2q = op("gpsimd", lambda e:
                       e.dma_start(out=wbuf2[:, :, 0:1024], in_=wqkva_r[:, :, 0:1024]), [(dW, 16)])
            op("gpsimd", lambda e:
               e.dma_start(out=wbuf2[:, :, 1024:2048], in_=wouta_r[:, :, :]), [(dW, 16)])
            c_w2o = dW.n

            evmulA = {}
            recipA = {}

            def denA_s1(ph):
                wait("scalar", tM, tM.n)
                wait("scalar", tP, tP.n)
                r0 = op("scalar", lambda e, ph=ph:
                        e.activation(rowtmp[0:1, ph % 2, :], psB[ph % 2][64:65, :], AF.Ln), [(sM, 1)])
                wait("scalar", sM, r0)
                recipA[ph] = op("scalar", lambda e, ph=ph:
                                e.activation(hrow2[0:1, ph % 2, :], rowtmp[0:1, ph % 2, :],
                                             AF.Exp, scale=-1.0), [(sM, 1)])

            def denA_s2(ph, qg):
                wait("tensor", sM, recipA[ph])
                wait("tensor", vM, psc_free[0])
                bc = mm(psC[0:64, :], onesb[0:1, 0:64], hrow2[0:1, ph % 2, :], True, True, [(tP, 1)])
                wait("vector", tP, bc)
                base, ch = 64 * (ph % 2), ph // 2
                d3 = op("vector", lambda e, base=base, ch=ch, ph=ph, qg=qg:
                        e.tensor_copy(atno[base:base + 64, qg % 2, ch, :], psB[ph % 2][0:64, :]), [(vM, 1)])
                wait("vector", vM, d3)
                evmulA[ph] = op("vector", lambda e, base=base, ch=ch, qg=qg:
                                e.tensor_tensor(atno[base:base + 64, qg % 2, ch, :],
                                                atno[base:base + 64, qg % 2, ch, :],
                                                psC[0:64, :], OP.mult), [(vM, 1)])
                psc_free[0] = evmulA[ph]

            gi = 0
            exp_done = {}
            pv_done = {}
            jA_ = [0]
            Acopy = {}
            dmaA = {}
            pend2 = []
            for qg in range(NQG):
                for qc in range(NC):
                    sl = psa_slot(jk)
                    wait("tensor", dW, c_w2q)
                    if jk >= 4:
                        wait("tensor", vM, evacA[jk - 4])
                    g = mm_group(sl, [(wbuf2[:, c, qc * 128:(qc + 1) * 128],
                                       xn[:, c, qg * 512:(qg + 1) * 512]) for c in range(NC)],
                                 [(tM, 1)])
                    wait("vector", tM, g)
                    evacA[jk] = op("vector", lambda e, sl=sl, qc=qc:
                                   e.tensor_copy(Qa[:, qc, :], sl), [(vM, 1)])
                    jk += 1
                if DEBUG and s == 0 and qg == 0:
                    wait("gpsimd", vM, vM.n)
                    op("gpsimd", lambda e: e.dma_start(out=dbg_ka, in_=Ka[:, :, :]), [(dX, 16)])
                    op("gpsimd", lambda e: e.dma_start(out=dbg_qa, in_=Qa[:, :, :]), [(dX, 16)])
                    op("gpsimd", lambda e: e.dma_start(out=dbg_va, in_=Va[:, :, :]), [(dX, 16)])
                NPAIR = NTT // 2
                items = [(h, kp) for h in range(HA) for kp in range(NPAIR)]

                def emit_pv(idx2):
                    h2_, kp2 = items[idx2]
                    gi2 = gi0 + idx2
                    wait("tensor", sM, exp_done[gi2])
                    if kp2 == 0 and h2_ >= 2:
                        wait("tensor", vM, evmulA[h2_ - 2])
                    last = 0
                    for half in range(2):
                        kt = 2 * kp2 + half
                        last = mm(psB[h2_ % 2][0:65, :], Va[:, kt, 65 * h2_:65 * h2_ + 65],
                                  Pb2[:, gi2 % 2, half * 512:(half + 1) * 512],
                                  (kp2 == 0 and half == 0), (kp2 == NPAIR - 1 and half == 1),
                                  [(tM, 1)] if half == 1 else ())
                    pv_done[gi2] = last
                    if kp2 == NPAIR - 1:
                        denA_s1(h2_)
                        pendA.append((h2_, idx2))

                gi0 = gi
                pendA = []
                for idx, (h, kp) in enumerate(items):
                    base, ch = 64 * (h % 2), h // 2
                    if idx == 0:
                        wait("tensor", vM, vM.n)
                    if gi >= 2:
                        wait("tensor", sM, exp_done[gi - 2])
                    stp = 0
                    for half in range(2):
                        kt = 2 * kp + half
                        stp = mm(psA[gi % 2][:, half * 512:(half + 1) * 512],
                                 Ka[base:base + 64, ch, kt * 128:(kt + 1) * 128],
                                 Qa[base:base + 64, ch, :], True, True,
                                 [(tM, 1)] if half == 1 else ())
                    if pendA and idx - pendA[0][1] >= 2:
                        denA_s2(pendA.pop(0)[0], qg)
                    if idx >= 1:
                        emit_pv(idx - 1)
                    wait("scalar", tM, stp)
                    if gi >= 2:
                        wait("scalar", tM, pv_done[gi - 2])
                    exp_done[gi] = op("scalar", lambda e, gi=gi:
                                      e.activation(Pb2[:, gi % 2, :], psA[gi % 2][:, :],
                                                   AF.Exp, scale=1.0 / 32.0), [(sM, 1)])
                    if DEBUG and s == 0 and qg == 0 and gi == 0:
                        wait("gpsimd", sM, exp_done[0])
                        op("gpsimd", lambda e: e.dma_start(out=dbg_p, in_=Pb2[:, :, :]), [(dX, 16)])
                    gi += 1
                emit_pv(len(items) - 1)
                while pendA:
                    denA_s2(pendA.pop(0)[0], qg)
                if DEBUG and s == 0 and qg == 0:
                    wait("gpsimd", vM, vM.n)
                    op("gpsimd", lambda e: e.dma_start(out=dbg_atno, in_=atno[:, :, :]), [(dX, 16)])
                def mk_Amm(tt, og, qg=qg):
                    def emit():
                        jA = jA_[0]
                        wait("tensor", vM, evmulA[14])
                        wait("tensor", vM, evmulA[15])
                        wait("tensor", vM, psc_free[0])
                        wait("tensor", dW, c_w2o)
                        g = mm_group(psC[:, :],
                                     [(atno[:, qg % 2, c, (tt % 4) * 128:(tt % 4 + 1) * 128],
                                       wbuf2[:, c, 1024 + og * 512:1024 + (og + 1) * 512])
                                      for c in range(NC)], [(tM, 1)])
                        wait("vector", tM, g)
                        if jA >= 2:
                            wait("vector", dOA[jA % 2], dmaA[jA - 2])
                        Acopy[jA] = op("vector", lambda e, j=jA:
                                       e.tensor_copy(tmp2[:, j % 4, :], psC[:, :]), [(vM, 1)])
                        psc_free[0] = Acopy[jA]
                        wait("sync", vM, Acopy[jA])
                        dmaA[jA] = op("sync", lambda e, s=s, tt=tt, og=og, j=jA:
                                      e.dma_start(out=outr[:, s, tt, og * 512:(og + 1) * 512],
                                                  in_=tmp2[:, j % 4, :]), [(dOA[jA % 2], 16)])
                        jA_[0] = jA + 1
                    return emit

                for tt4 in range(4):
                    for og in range(2):
                        pend2.append(mk_Amm(qg * 4 + tt4, og))
            while pend2:
                pend2.pop(0)()
            a_writes_done = (dOA[0].n, dOA[1].n)

            wait("gpsimd", tM, tM.n)
            wait("gpsimd", vM, vM.n)
            wait("gpsimd", sM, sM.n)
            c_wfkv = op("gpsimd", lambda e:
                        e.dma_start(out=wqkvf[:, :, 0:512], in_=qkvwf_r[:, :, 1024:1536]), [(dW, 16)])
            c_wproj = op("gpsimd", lambda e:
                         e.dma_start(out=wproj[:, :, :], in_=projw_r[:, :, :]), [(dW, 16)])
            op("gpsimd", lambda e: e.dma_start(out=qkvbrow[0:1, :], in_=qkvbf_d), [(dW, 16)])
            c_brow = op("gpsimd", lambda e: e.dma_start(out=projbrow[0:1, :], in_=projb_d), [(dW, 16)])
            wait("sync", vM, vM.n)
            wait("sync", tM, tM.n)
            op("sync", lambda e: e.dma_start(out=cosb[:, :], in_=cosq_d), [(dC, 16)])
            op("sync", lambda e: e.dma_start(out=sinb[:, :], in_=sinq_d), [(dC, 16)])
            c_msk = op("sync", lambda e: e.dma_start(
                out=maskb[:, :, :], in_=maskf_d.rearrange("a p b -> p a b")), [(dC, 16)])
            c_cos = c_msk

            hb_done = {}
            for qg in range(NQG):
                xq_mm = {}
                xq_done = {}
                for c in range(NC):
                    if c >= 2:
                        wait("vector", tM, xq_mm[c - 2])
                    xq_done[c] = op("vector", lambda e, c=c, qg=qg:
                                    e.tensor_tensor(xnsq[:, c % 2, :],
                                                    xn[:, c, qg * 512:(qg + 1) * 512],
                                                    xn[:, c, qg * 512:(qg + 1) * 512], OP.mult), [(vM, 1)])
                    if c == 0:
                        wait("tensor", vM, psc_free[0])
                    wait("tensor", vM, xq_done[c])
                    xq_mm[c] = mm(psC[0:1, :], onescol[:, :], xnsq[:, c % 2, :],
                                  c == 0, c == NC - 1, [(tM, 1)] if c == NC - 1 else ())
                wait("vector", tM, xq_mm[NC - 1])
                rv = op("vector", lambda e:
                        e.tensor_scalar_mul(rowtmp[0:1, 0, :], psC[0:1, :], 1.0 / D), [(vM, 1)])
                psc_free[0] = rv
                wait("scalar", vM, rv)
                l1 = op("scalar", lambda e:
                        e.activation(rowtmp[0:1, 1, :], rowtmp[0:1, 0, :], AF.Ln, bias=epsb[0:1, 0:1]), [(sM, 1)])
                wait("scalar", sM, l1)
                l2 = op("scalar", lambda e:
                        e.activation(hrow[0:1, :], rowtmp[0:1, 1, :], AF.Exp, scale=-0.5), [(sM, 1)])
                wait("tensor", sM, l2)
                wait("tensor", vM, psc_free[0])
                bc = mm(psC[:, :], onesb[0:1, 0:128], hrow[0:1, :], True, True, [(tP, 1)])
                wait("vector", tP, bc)
                for c in range(NC):
                    op("vector", lambda e, c=c, qg=qg:
                       e.scalar_tensor_tensor(hb[:, c, qg * 512:(qg + 1) * 512],
                                              xn[:, c, qg * 512:(qg + 1) * 512],
                                              anw[:, c:c + 1], psC[:, :], OP.mult, OP.mult),
                       [(vM, 1)])
                psc_free[0] = vM.n
                hb_done[qg] = vM.n
            h_done_v = vM.n
            if DEBUG and s == 0:
                wait("gpsimd", vM, vM.n)
                op("gpsimd", lambda e: e.dma_start(out=dbg_hb, in_=hb[:, :, :]), [(dX, 16)])

            evac3 = {}
            jk3 = 0
            for qg in range(NQG):
                for kc in range(2):
                    sl = psa_slot(jk3)
                    wait("tensor", dW, c_brow)
                    wait("tensor", vM, hb_done[qg])
                    if jk3 >= 4:
                        wait("tensor", vM, evac3[jk3 - 4])
                    pairs = [(wqkvf[:, c, kc * 128:(kc + 1) * 128],
                              hb[:, c, qg * 512:(qg + 1) * 512]) for c in range(NC)]
                    pairs.append((qkvbrow[0:1, 1024 + kc * 128:1024 + (kc + 1) * 128],
                                  onesb[0:1, :]))
                    g = mm_group(sl, pairs, [(tM, 1)])
                    wait("vector", tM, g)
                    evac3[jk3] = op("vector", lambda e, sl=sl, kc=kc, qg=qg:
                                    e.tensor_copy(Kf[:, kc, qg * 512:(qg + 1) * 512], sl), [(vM, 1)])
                    jk3 += 1
            op("vector", lambda e: e.memset(
                Vf[:, :, :].rearrange("p t (h x) -> p t h x", x=65)[:, :, :, 64:65], 1.0), [(vM, 1)])
            jb3 = 0
            evacB3 = {}
            for tt in range(NTT):
                pb = psB[jb3 % 2]
                if jb3 >= 2:
                    wait("tensor", vM, evacB3[jb3 - 2])
                pairs = [(hb[:, c, tt * 128:(tt + 1) * 128], wqkvf[:, c, 256:512])
                         for c in range(NC)]
                pairs.append((onesb[0:1, 0:128], qkvbrow[0:1, 1280:1536]))
                g = mm_group(pb[:, 0:256], pairs, [(tM, 1)])
                wait("vector", tM, g)
                dst = Vf[:, tt, :].rearrange("p (a b) -> p a b", a=KVH, b=65)[:, :, 0:64]
                evacB3[jb3] = op("vector", lambda e, pb=pb, dst=dst:
                                 e.tensor_copy(dst, pb[:, 0:256].rearrange("p (a b) -> p a b", a=KVH, b=64)),
                                 [(vM, 1)])
                jb3 += 1

            wait("vector", dC, c_cos)
            for kc in range(2):
                for colh in range(2):
                    kap = Kf[:, kc, colh * 1024:(colh + 1) * 1024]
                    csl = cosb[:, colh * 1024:(colh + 1) * 1024]
                    snl = sinb[:, colh * 1024:(colh + 1) * 1024]
                    sw0 = swp[:, 0, :]
                    sw1 = swp[:, 1, :]
                    last_cp = 0
                    for blk in range(4):
                        srow = [32, 0, 96, 64][blk]
                        drow = blk * 32
                        last_cp = op("vector", lambda e, kap=kap, srow=srow, drow=drow:
                                     e.tensor_copy(swp[drow:drow + 32, 0, :],
                                                   kap[srow:srow + 32, :]), [(vM, 1)])
                    wait("vector", vM, last_cp)
                    u1 = op("vector", lambda e, kap=kap, csl=csl, sw1=sw1:
                            e.tensor_tensor(sw1, kap, csl, OP.mult), [(vM, 1)])
                    u2 = op("vector", lambda e, snl=snl, sw0=sw0:
                            e.tensor_tensor(sw0, sw0, snl, OP.mult), [(vM, 1)])
                    wait("vector", vM, u2)
                    op("vector", lambda e, kap=kap, sw0=sw0, sw1=sw1:
                       e.tensor_tensor(kap, sw1, sw0, OP.add), [(vM, 1)])
            krope_done = vM.n
            if DEBUG and s == 0:
                wait("gpsimd", vM, vM.n)
                op("gpsimd", lambda e: e.dma_start(out=dbg_kf, in_=Kf[:, :, :]), [(dX, 16)])

            evmulF = {}
            recipF = {}

            def denF_s1(sl_, hname):
                wait("vector", tM, tM.n)
                wait("vector", tP, tP.n)
                d1 = op("vector", lambda e, hname=hname, sl_=sl_:
                        e.tensor_scalar_add(rowtmp[0:1, sl_ % 2, :], psB[sl_ % 2][64:65, :],
                                            sinkexp[0:1, hname:hname + 1]), [(vM, 1)])
                wait("scalar", vM, d1)
                r0 = op("scalar", lambda e, sl_=sl_:
                        e.activation(rowtmp[0:1, sl_ % 2, :], rowtmp[0:1, sl_ % 2, :], AF.Ln), [(sM, 1)])
                wait("scalar", sM, r0)
                recipF[sl_] = op("scalar", lambda e, sl_=sl_:
                                 e.activation(hrow2[0:1, sl_ % 2, :], rowtmp[0:1, sl_ % 2, :],
                                              AF.Exp, scale=-1.0), [(sM, 1)])

            def denF_s2(sl_):
                base, ch = 64 * (sl_ % 2), sl_ // 2
                wait("tensor", sM, recipF[sl_])
                wait("tensor", vM, psc_free[0])
                bc = mm(psC[0:64, :], onesb[0:1, 0:64], hrow2[0:1, sl_ % 2, :], True, True, [(tP, 1)])
                wait("vector", tP, bc)
                d3 = op("vector", lambda e, base=base, ch=ch, sl_=sl_:
                        e.tensor_copy(yf[base:base + 64, ch, :], psB[sl_ % 2][0:64, :]), [(vM, 1)])
                wait("vector", vM, d3)
                evmulF[sl_] = op("vector", lambda e, base=base, ch=ch:
                                 e.tensor_tensor(yf[base:base + 64, ch, :],
                                                 yf[base:base + 64, ch, :],
                                                 psC[0:64, :], OP.mult), [(vM, 1)])
                psc_free[0] = evmulF[sl_]

            gi3 = 0
            fexp = {}
            fmask = {}
            fpv = {}
            first_xxf = [True]
            dma_xxf = {}
            pend3 = []
            psT_free = [p1e[NTT - 1]]
            wait("gpsimd", tM, tM.n)
            c_wfq = op("gpsimd", lambda e:
                       e.dma_start(out=wqkvf[:, :, 0:1024], in_=qkvwf_r[:, :, 0:1024]),
                       [(dW, 16)])
            qrope_q = {}
            jk3_box = [jk3]

            def emit_qblock(qgq, psa_guard):
                jk3q = jk3_box[0]
                for qc in range(NC):
                    sl = psa_slot(jk3q)
                    wait("tensor", dW, c_wfq)
                    if qc == 0 and psa_guard:
                        wait("tensor", sM, psa_guard)
                    if jk3q >= 4:
                        wait("tensor", vM, evac3[jk3q - 4])
                    pairs = [(wqkvf[:, c, qc * 128:(qc + 1) * 128],
                              hb[:, c, qgq * 512:(qgq + 1) * 512]) for c in range(NC)]
                    pairs.append((qkvbrow[0:1, qc * 128:(qc + 1) * 128], onesb[0:1, :]))
                    g = mm_group(sl, pairs, [(tM, 1)])
                    wait("vector", tM, g)
                    evac3[jk3q] = op("vector", lambda e, sl=sl, qc=qc:
                                     e.tensor_copy(Qf[:, qc, :], sl), [(vM, 1)])
                    jk3q += 1
                jk3_box[0] = jk3q
                for qc in range(NC):
                    qap = Qf[:, qc, :]
                    cs = cosb[:, qgq * 512:(qgq + 1) * 512]
                    sn = sinb[:, qgq * 512:(qgq + 1) * 512]
                    sw0 = swp[:, 0, 0:512]
                    sw1 = swp[:, 1, 0:512]
                    last_cp = 0
                    for blk in range(4):
                        srow = [32, 0, 96, 64][blk]
                        drow = blk * 32
                        last_cp = op("vector", lambda e, qap=qap, srow=srow, drow=drow:
                                     e.tensor_copy(swp[drow:drow + 32, 0, 0:512],
                                                   qap[srow:srow + 32, :]), [(vM, 1)])
                    wait("vector", vM, last_cp)
                    u1 = op("vector", lambda e, qap=qap, cs=cs, sw1=sw1:
                            e.tensor_tensor(sw1, qap, cs, OP.mult), [(vM, 1)])
                    u2 = op("vector", lambda e, sn=sn, sw0=sw0:
                            e.tensor_tensor(sw0, sw0, sn, OP.mult), [(vM, 1)])
                    wait("vector", vM, u2)
                    op("vector", lambda e, qap=qap, sw0=sw0, sw1=sw1:
                       e.tensor_tensor(qap, sw1, sw0, OP.add), [(vM, 1)])
                qrope_q[qgq] = vM.n

            emit_qblock(0, 0)
            for qg in range(NQG):
                qrope_done = qrope_q[qg]

                kts = [k for k in range(4 * qg - 1, 4 * qg + 4) if 0 <= k < NTT]
                items3 = [(slp, kt, half)
                          for slp in range(HF // 2) for kt in kts for half in range(2)]
                gi30 = gi3

                def slotA(h):
                    return psA[(h % 4) // 2][:, (h % 2) * 512:(h % 2 + 1) * 512]

                def slotP(h):
                    return Pb3[:, h % 4, :]

                def emit_pv3(idx2):
                    slp2, kt2, half2 = items3[idx2]
                    h2_ = gi30 + idx2
                    wait("tensor", vM, fmask[h2_])
                    if kt2 == kts[0] and slp2 >= 1:
                        slprev = 2 * (slp2 - 1) + half2
                        while pendF and pendF[0][0] <= slprev:
                            denF_s2(pendF.pop(0)[0])
                        wait("tensor", vM, evmulF[slprev])
                    sl2 = 2 * slp2 + half2
                    kv2 = HEAD_ORDER[sl2] // 4
                    fpv[h2_] = mm(psB[half2][0:65, :], Vf[:, kt2, 65 * kv2:65 * kv2 + 65],
                                  slotP(h2_), kt2 == kts[0], kt2 == kts[-1], [(tM, 1)])
                    if kt2 == kts[-1]:
                        denF_s1(sl2, HEAD_ORDER[sl2])
                        pendF.append((sl2, h2_))

                pendF = []
                for idx, (slp, kt, half) in enumerate(items3):
                    j = kt - 4 * qg + 1
                    if idx == 0:
                        wait("tensor", vM, qrope_done)
                        wait("tensor", vM, krope_done)
                        wait("tensor", dC, c_msk)
                    if gi3 >= 4:
                        wait("tensor", sM, fexp[gi3 - 4])
                    sl_ = 2 * slp + half
                    kv = HEAD_ORDER[sl_] // 4
                    kch, kbase = kv // 2, 64 * (kv % 2)
                    stm = mm(slotA(gi3),
                             Kf[kbase:kbase + 64, kch, kt * 128:(kt + 1) * 128],
                             Qf[64 * half:64 * half + 64, slp, :], True, True,
                             [(tM, 1)])
                    if pendF and gi3 - pendF[0][1] >= 4:
                        denF_s2(pendF.pop(0)[0])
                    if pend3 and idx % 8 == 7:
                        pend3.pop(0)()
                    if idx >= 1:
                        emit_pv3(idx - 1)
                    wait("scalar", tM, stm)
                    if gi3 >= 4:
                        wait("scalar", tM, fpv[gi3 - 4])
                    fexp[gi3] = op("scalar", lambda e, gi3=gi3:
                                   e.activation(slotP(gi3), slotA(gi3),
                                                AF.Exp, scale=1.0 / 8.0), [(sM, 1)])
                    wait("vector", sM, fexp[gi3])
                    fmask[gi3] = op("vector", lambda e, gi3=gi3, j=j:
                                    e.tensor_tensor(slotP(gi3), slotP(gi3),
                                                    maskb[:, j, :], OP.mult), [(vM, 1)])
                    gi3 += 1
                emit_pv3(len(items3) - 1)
                while pendF:
                    denF_s2(pendF.pop(0)[0])

                if DEBUG and s == 0 and qg == 0:
                    wait("gpsimd", vM, vM.n)
                    op("gpsimd", lambda e: e.dma_start(out=dbg_yf, in_=yf[:, :, :]), [(dX, 16)])
                if qg < NQG - 1:
                    emit_qblock(qg + 1, fexp[gi3 - 1])
                xf_v = 0
                for ofc in range(NC):
                    jk3p = jk3_box[0]
                    sl = psa_slot(jk3p)
                    wait("tensor", dW, c_wproj)
                    if qg == NQG - 1 and ofc == 0:
                        wait("tensor", sM, fexp[gi3 - 1])
                    if jk3p >= 4:
                        wait("tensor", vM, evac3[jk3p - 4])
                    pairs = [(wproj[:, c, ofc * 128:(ofc + 1) * 128], yf[:, c, :]) for c in range(NC)]
                    pairs.append((projbrow[0:1, ofc * 128:(ofc + 1) * 128], onesb[0:1, :]))
                    g = 0
                    for ci, (lf, rf) in enumerate(pairs):
                        if ofc == 0 and ci < NC:
                            wait("tensor", vM, evmulF[2 * ci + 1])
                        g = mm(sl, lf, rf, ci == 0, ci == len(pairs) - 1,
                               [(tM, 1)] if ci == len(pairs) - 1 else ())
                    wait("vector", tM, g)
                    evac3[jk3p] = op("vector", lambda e, sl=sl, ofc=ofc, qg=qg:
                                     e.tensor_tensor(xn[:, ofc, qg * 512:(qg + 1) * 512],
                                                     xn[:, ofc, qg * 512:(qg + 1) * 512],
                                                     sl, OP.add), [(vM, 1)])
                    xf_v = evac3[jk3p]
                    jk3_box[0] = jk3p + 1
                xf_done = vM.n

                def mk_rstdh2(qg=qg, xf_done=xf_done):
                    def emit():
                        fq_mm = {}
                        fq = {}
                        for c in range(NC + 1):
                            if c < NC:
                                if c == 0:
                                    wait("vector", vM, xf_done)
                                if c >= 2:
                                    wait("vector", tM, fq_mm.get(c - 2, 0))
                                fq[c] = op("vector", lambda e, c=c, qg=qg:
                                           e.tensor_tensor(xfsq[:, c % 2, :],
                                                           xn[:, c, qg * 512:(qg + 1) * 512],
                                                           xn[:, c, qg * 512:(qg + 1) * 512],
                                                           OP.mult), [(vM, 1)])
                            if c >= 1:
                                cc = c - 1
                                if cc == 0:
                                    wait("tensor", vM, psc_free[0])
                                wait("tensor", vM, fq[cc])
                                fq_mm[cc] = mm(psC[0:1, :], onescol[:, :], xfsq[:, cc % 2, :],
                                               cc == 0, cc == NC - 1,
                                               [(tM, 1)] if cc == NC - 1 else ())
                        wait("vector", tM, fq_mm[NC - 1])
                        rv2 = op("vector", lambda e:
                                 e.tensor_scalar_mul(rstrow[0:1, 0, :], psC[0:1, :], 1.0 / D), [(vM, 1)])
                        psc_free[0] = rv2
                        wait("scalar", vM, rv2)
                        m1 = op("scalar", lambda e:
                                e.activation(rstrow[0:1, 1, :], rstrow[0:1, 0, :], AF.Ln,
                                             bias=epsb[0:1, 0:1]), [(sM, 1)])
                        wait("scalar", sM, m1)
                        op("scalar", lambda e, qg=qg:
                           e.activation(rstdh2[0:1, qg * 512:(qg + 1) * 512], rstrow[0:1, 1, :],
                                        AF.Exp, scale=-0.5), [(sM, 1)])
                    return emit

                def mk_assembly(t, xf_done=xf_done):
                    def emit():
                        if first_xxf[0]:
                            first_xxf[0] = False
                            wait("gpsimd", dOA[0], a_writes_done[0])
                            wait("gpsimd", dOA[1], a_writes_done[1])
                        wait("sync", vM, vM.n)
                        wait("sync", dX3, dX3.n)
                        ldx = op("sync", lambda e, s=s, t=t:
                                 e.dma_start(out=xbuf3[:, :], in_=xr[:, s, t, :]), [(dX3, 16)])
                        wait("tensor", vM, xf_done)
                        wait("tensor", vM, psT_free[0])
                        trx = 0
                        for c in range(NC):
                            trx = op("tensor", lambda e, c=c, t=t:
                                     e.transpose(psT[:, c * 128:(c + 1) * 128],
                                                 xn[:, c, t * 128:(t + 1) * 128], ident[:, :]),
                                     [(tM, 1)] if c == NC - 1 else ())
                        wait("vector", tM, trx)
                        wait("vector", dX3, ldx)
                        if t >= 2:
                            wait("vector", dOX[t % 2], dma_xxf[t - 2])
                        vv = op("vector", lambda e, t=t:
                                e.tensor_tensor(tmp3[:, t % 2, :], xbuf3[:, :],
                                                psT[:, :], OP.add), [(vM, 1)])
                        psT_free[0] = vv
                        wait("gpsimd", vM, vv)
                        dma_xxf[t] = op("gpsimd", lambda e, s=s, t=t:
                                        e.dma_start(out=outr[:, s, t, :], in_=tmp3[:, t % 2, :],
                                                    accum_op=OP.add), [(dOX[t % 2], 16)])
                    return emit

                pend3.append(mk_rstdh2())
                for tt4 in range(4):
                    pend3.append(mk_assembly(qg * 4 + tt4))
            while pend3:
                pend3.pop(0)()
            xxf_done = (dOX[0].n, dOX[1].n)
            if DEBUG and s == 0:
                wait("gpsimd", vM, vM.n)
                wait("gpsimd", sM, sM.n)
                op("gpsimd", lambda e: e.dma_start(out=dbg_xf, in_=xn[:, :, :]), [(dX, 16)])
                op("gpsimd", lambda e: e.dma_start(out=dbg_rstdh2, in_=rstdh2[0:1, :]), [(dX, 16)])

            wait("gpsimd", tM, tM.n)
            wait("gpsimd", vM, vM.n)
            wait("gpsimd", sM, sM.n)
            for qg in range(NQG):
                wait("tensor", vM, psc_free[0])
                wait("tensor", sM, sM.n)
                bc = mm(psC[:, :], onesb[0:1, 0:128], rstdh2[0:1, qg * 512:(qg + 1) * 512],
                        True, True, [(tP, 1)])
                wait("vector", tP, bc)
                for c in range(NC):
                    op("vector", lambda e, c=c, qg=qg:
                       e.scalar_tensor_tensor(h2[:, c, qg * 512:(qg + 1) * 512],
                                              xn[:, c, qg * 512:(qg + 1) * 512],
                                              fnw[:, c:c + 1], psC[:, :], OP.mult, OP.mult),
                       [(vM, 1)])
                psc_free[0] = vM.n
            h2_done = vM.n

            gcopy = {}
            for t in range(NTT):
                pb = psB[t % 2]
                wait("tensor", vM, h2_done)
                if t >= 2:
                    wait("tensor", vM, gcopy[t - 2])
                pairs = [(h2[:, c, t * 128:(t + 1) * 128], gatew[:, c, :]) for c in range(NC)]
                pairs.append((onesb[0:1, 0:128], gateb[0:1, :]))
                g = mm_group(pb[:, 0:NE], pairs, [(tM, 1)])
                wait("vector", tM, g)
                gcopy[t] = op("vector", lambda e, pb=pb, t=t:
                              e.tensor_copy(gwork[:, t, :], pb[:, 0:NE]), [(vM, 1)])
            rm = {}; req = {}; rmask = {}; rexp = {}; rsum = {}; rrec = {}; rsc = {}; rcomb = {}
            for r in range(4):
                for t in range(NTT):
                    if r == 0:
                        wait("vector", vM, gcopy[t])
                    else:
                        wait("vector", vM, rmask[(r - 1, t)])
                    rm[(r, t)] = op("vector", lambda e, t=t, r=r:
                                    e.reduce_max(gm[:, t, r:r + 1], gwork[:, t, :], axis=AX.X), [(vM, 1)])
                for t in range(NTT):
                    wait("vector", vM, rm[(r, t)])
                    req[(r, t)] = op("vector", lambda e, t=t, r=r:
                                     e.tensor_scalar(geq[:, t, r, :], gwork[:, t, :], gm[:, t, r:r + 1],
                                                     None, OP.is_equal), [(vM, 1)])
                if r < 3:
                    for t in range(NTT):
                        wait("vector", vM, req[(r, t)])
                        rmask[(r, t)] = op("vector", lambda e, t=t, r=r:
                                           e.scalar_tensor_tensor(gwork[:, t, :], geq[:, t, r, :], -1e30,
                                                                  gwork[:, t, :], OP.mult, OP.add), [(vM, 1)])
            for t in range(NTT):
                wait("scalar", vM, req[(3, t)])
                rexp[t] = op("scalar", lambda e, t=t:
                             e.activation(gm[:, t, 4:8], gm[:, t, 0:4], AF.Exp), [(sM, 1)])
            for t in range(NTT):
                wait("vector", sM, rexp[t])
                rsum[t] = op("vector", lambda e, t=t:
                             e.reduce_sum(gm[:, t, 3:4], gm[:, t, 4:8], axis=AX.X), [(vM, 1)])
            for t in range(NTT):
                wait("vector", vM, rsum[t])
                rrec[t] = op("vector", lambda e, t=t:
                             e.reciprocal(gm[:, t, 2:3], gm[:, t, 3:4]), [(vM, 1)])
            for t in range(NTT):
                wait("vector", vM, rrec[t])
                rsc[t] = op("vector", lambda e, t=t:
                            e.tensor_scalar_mul(gm[:, t, 4:8], gm[:, t, 4:8], gm[:, t, 2:3]), [(vM, 1)])
            for t in range(NTT):
                wait("vector", vM, rsc[t])
                op("vector", lambda e, t=t, s=s:
                   e.tensor_copy(gwsave[:, s, t, :], gm[:, t, 4:8]), [(vM, 1)])
            gv = gwork[:, :, :]
            w1 = op("vector", lambda e: e.tensor_tensor(gv, geq[:, :, 0, :], geq[:, :, 1, :], OP.add), [(vM, 1)])
            wait("vector", vM, w1)
            w2 = op("vector", lambda e: e.tensor_tensor(gv, gv, geq[:, :, 2, :], OP.add), [(vM, 1)])
            wait("vector", vM, w2)
            w3 = op("vector", lambda e: e.tensor_tensor(gv, gv, geq[:, :, 3, :], OP.add), [(vM, 1)])
            wait("vector", vM, w3)
            cind = op("vector", lambda e: e.tensor_copy(indb[:, :, :], gv), [(vM, 1)])
            wait("tensor", vM, psc_free[0])
            wait("tensor", vM, cind)
            wait("tensor", dC, c_lt)
            cmm = 0
            for t in range(NTT):
                cmm = mm(psC[0:32, t:t + 1], indb[:, t, :], onescol[:, :], True, True,
                         [(tM, 1)] if t == NTT - 1 else ())
            wait("vector", tM, cmm)
            cevac = op("vector", lambda e: e.tensor_copy(cntsb[0:32, 0:16], psC[0:32, 0:16]), [(vM, 1)])
            psc_free[0] = cevac
            wait("gpsimd", vM, cevac)
            cd1 = op("gpsimd", lambda e: e.dma_start(out=cnt_d, in_=cntsb[0:32, 0:16]), [(dW, 16)])
            wait("gpsimd", dW, cd1)
            cd2 = op("gpsimd", lambda e: e.dma_start(
                out=cntC[0:16, 0:32], in_=cnt_d.rearrange("e t -> t e"),
                allow_slow_non_contiguous=True), [(dW, 16)])
            wait("tensor", dW, cd2)
            mm(psB[0][0:16, 0:32], lt16b[:, :], cntC[0:16, 0:32], True, False)
            bmm = mm(psB[0][0:16, 0:32], onesb[0:1, 0:16], eoffb[0:1, 32 * s:32 * s + 32],
                     False, True, [(tM, 1)])
            wait("vector", tM, bmm)
            bevac = op("vector", lambda e: e.tensor_copy(basesb[0:16, 0:32], psB[0][0:16, 0:32]), [(vM, 1)])
            wait("gpsimd", vM, bevac)
            bd1 = op("gpsimd", lambda e: e.dma_start(out=base_d, in_=basesb[0:16, 0:32]), [(dW, 16)])
            wait("gpsimd", dW, bd1)
            bd2 = op("gpsimd", lambda e: e.dma_start(
                out=baserow[0:1, :], in_=base_d.rearrange("t e -> (t e)").unsqueeze(0)), [(dW, 16)])
            wait("tensor", dW, bd2)
            wait("tensor", vM, psc_free[0])
            bfmm = mm(psC[:, :], onesf[0:1, :], baserow[0:1, :], True, True, [(tP, 1)])
            wait("vector", tP, bfmm)
            bfev = op("vector", lambda e: e.tensor_copy(bfield[:, :], psC[:, :]), [(vM, 1)])
            psc_free[0] = bfev
            if s == SPC - 1:
                pf_wug = op("gpsimd", lambda e:
                            e.dma_start(out=wu[:, 0, :, :], in_=wup_r[:, 0, :, 1024:2048]), [(dWug, 16)])
                pf_wuu = op("gpsimd", lambda e:
                            e.dma_start(out=wu[:, 1, :, :], in_=wup_r[:, 0, :, 0:1024]), [(dWuu, 16)])
                pf_wd = op("gpsimd", lambda e:
                           e.dma_start(out=wd[:, :, :], in_=wdn_r[:, 0, :, :]), [(dWdd, 16)])
            sc_done = {}
            sf_done = {}
            trh_done = {}
            ev_done = {}
            for t in range(NTT):
                pbr = psB[t % 2]
                if t >= 2:
                    wait("tensor", vM, sf_done[t - 2])
                else:
                    wait("tensor", vM, bfev)
                rmm = mm(pbr[:, 0:NE], lt128b[:, :], indb[:, t, :], True, True, [(tM, 1)])
                wait("vector", tM, rmm)
                wait("vector", dSC[t % 2], sc_done.get(t - 2, 0))
                sf0 = op("vector", lambda e, pbr=pbr, t=t:
                         e.tensor_tensor(sfield[:, :], pbr[:, 0:NE], bfield[:, 32 * t:32 * t + 32],
                                         OP.add), [(vM, 1)])
                sf_done[t] = sf0
                last = sf0
                for r in range(4):
                    wait("vector", vM, last)
                    m1_ = op("vector", lambda e, t=t, r=r:
                             e.tensor_tensor(gwork[:, t, :], geq[:, t, r, :], sfield[:, :], OP.mult),
                             [(vM, 1)])
                    wait("vector", vM, m1_)
                    last = op("vector", lambda e, t=t, r=r:
                              e.reduce_sum(sfl[:, t, r:r + 1], gwork[:, t, :], axis=AX.X), [(vM, 1)])
                wait("vector", vM, last)
                ci = op("vector", lambda e, t=t, s=s:
                        e.tensor_copy(selint[:, s, t, :], sfl[:, t, :]), [(vM, 1)])
                wait("tensor", vM, h2_done)
                if t >= 1:
                    wait("tensor", vM, ev_done[t - 1])
                trx = 0
                for c in range(NC):
                    trx = op("tensor", lambda e, c=c, t=t:
                             e.transpose(psT[:, c * 128:(c + 1) * 128],
                                         h2[:, c, t * 128:(t + 1) * 128], ident[:, :]),
                             [(tM, 1)] if c == NC - 1 else ())
                trh_done[t] = trx
                wait("vector", tM, trx)
                wait("vector", dSC[t % 2], sc_done.get(t - 2, 0))
                ev_done[t] = op("vector", lambda e, t=t:
                                e.tensor_copy(h2tm[:, t % 2, :], psT[:, :]), [(vM, 1)])
                wait("gpsimd", vM, ev_done[t])
                wait("gpsimd", vM, ci)
                last_sc = 0
                for r in range(4):
                    last_sc = op("gpsimd", lambda e, t=t, r=r, s=s:
                                 e.indirect_dma_start(
                                     out=gbuf_d,
                                     out_offset=IndirectOffsetOnAxis(selint[:, s, t, r:r + 1], 0),
                                     in_=h2tm[:, t % 2, :], in_offset=None), [(dSC[t % 2], 16)])
                sc_done[t] = last_sc
            seq_scatter_done[s] = (dSC[0].n, dSC[1].n)
            if DEBUG and s == 0:
                wait("gpsimd", vM, vM.n)
                op("gpsimd", lambda e: e.dma_start(out=dbg_h2, in_=h2[:, :, :]), [(dX, 16)])
                op("gpsimd", lambda e: e.dma_start(out=dbg_base, in_=baserow[0:1, :]), [(dX, 16)])
                op("gpsimd", lambda e: e.dma_start(out=dbg_sel[:, 0, :, :], in_=selint[:, 0, :, :]), [(dX, 16)])
                op("gpsimd", lambda e: e.dma_start(out=dbg_gw[:, 0, :, :], in_=gwsave[:, 0, :, :]), [(dX, 16)])
                op("gpsimd", lambda e: e.dma_start(out=dbg_geq, in_=geq[:, :, :, :]), [(dX, 16)])
                op("gpsimd", lambda e: e.dma_start(out=dbg_bf, in_=bfield[:, :]), [(dX, 16)])
                op("gpsimd", lambda e: e.dma_start(out=dbg_sfl, in_=sfl[:, :, :]), [(dX, 16)])
                wait("gpsimd", dW, dW.n)
                op("gpsimd", lambda e: e.dma_start(out=dbg_gbuf, in_=gbuf_d[0:256, :]), [(dX, 16)])

        wait("gpsimd", tM, tM.n)
        wait("gpsimd", vM, vM.n)
        wait("gpsimd", sM, sM.n)
        wug_loaded = pf_wug
        wuu_loaded = pf_wuu
        wd_loaded = pf_wd
        wait("sync", tM, tM.n)
        wait("sync", vM, vM.n)
        wait("sync", dSC[0], dSC[0].n)
        wait("sync", dSC[1], dSC[1].n)
        gtm_loaded = op("sync", lambda e:
                        e.dma_start(out=Gtm[:, 0, :, :],
                                    in_=gbuf_d[0:1024, :].rearrange("(t p) f -> p t f", p=128)),
                        [(dWgt, 16)])
        prev_umul_last = 0
        prev_down_t = 0

        def gtr_tt(e_, tt, load_cnt, st):
            gfm = G_fm1 if e_ % 2 == 0 else G_fm2
            if tt == 0:
                wait("tensor", dWgt, load_cnt)
            if st["prev"]:
                wait("tensor", vM, st["prev"])
            trx = 0
            for c in range(NC):
                trx = op("tensor", lambda e, c=c, tt=tt, e_=e_:
                         e.transpose(psT[:, c * 128:(c + 1) * 128],
                                     Gtm[:, e_ % 2, tt, c * 128:(c + 1) * 128], ident[:, :]),
                         [(tM, 1)] if c == NC - 1 else ())
            wait("vector", tM, trx)
            st["prev"] = op("vector", lambda e, tt=tt, gfm=gfm:
                            e.tensor_copy(gfm[:, :, tt * 128:(tt + 1) * 128],
                                          psT[:, :].rearrange("p (a b) -> p a b", a=NC, b=128)),
                            [(vM, 1)])

        def emit_gtr(e_, load_cnt, prev_gfm_use_t):
            st = {"prev": gfm_freed.get(e_ % 2, 0)}
            for tt in range(8):
                gtr_tt(e_, tt, load_cnt, st)
            return st["prev"], tM.n

        gfm_freed = {}
        gfm_ready = {}
        gfm_freed[0] = vM.n
        gfm_ready[0], gtr_t = emit_gtr(0, gtm_loaded, 0)
        wait("sync", tM, gtr_t)
        gtm_loaded = op("sync", lambda e:
                        e.dma_start(out=Gtm[:, 1, :, :],
                                    in_=gbuf_d[1024:2048, :]
                                    .rearrange("(t p) f -> p t f", p=128)), [(dWgt, 16)])
        for e_ in range(NE):
            G_fm = G_fm1 if e_ % 2 == 0 else G_fm2
            silu_done = {}
            for jj in range(8):
                tg, fp = jj // 4, jj % 4
                if jj == 0:
                    wait("tensor", dWug, wug_loaded)
                    wait("tensor", vM, gfm_ready[e_])
                    if e_ > 0:
                        wait("tensor", vM, prev_dmul_last)
                if jj >= 2:
                    wait("tensor", sM, silu_done[jj - 2])
                gph = 0
                for half in range(2):
                    ffc = 2 * fp + half
                    gph = mm_group(psA[jj % 2][:, half * 512:(half + 1) * 512],
                                   [(wu[:, 0, c, ffc * 128:(ffc + 1) * 128],
                                     G_fm[:, c, tg * 512:(tg + 1) * 512]) for c in range(NC)],
                                   [(tM, 1)] if half == 1 else ())
                wait("scalar", tM, gph)
                if jj == 0 and e_ > 0:
                    wait("scalar", tM, prev_down_t)
                silu_done[jj] = op("scalar", lambda e, jj=jj, fp=fp, tg=tg:
                                   e.activation(act[:, 2 * fp:2 * fp + 2, tg * 512:(tg + 1) * 512],
                                                psA[jj % 2][:, :], AF.Silu), [(sM, 1)])
            gphase_last = tM.n
            wait("gpsimd", tM, gphase_last)
            if e_ < NE - 1:
                wait("gpsimd", dWug, dWug.n)
                wug_loaded = op("gpsimd", lambda e, e_=e_:
                                e.dma_start(out=wu[:, 0, :, :],
                                            in_=wup_r[:, e_ + 1, :, 1024:2048]), [(dWug, 16)])
            umul = {}
            for jj in range(16):
                tg, ffc = jj // 8, jj % 8
                if jj == 0:
                    wait("tensor", dWuu, wuu_loaded)
                    if e_ > 0:
                        wait("tensor", vM, prev_umul_last)
                if jj >= 2:
                    wait("tensor", vM, umul[jj - 2])
                g = mm_group(psB[jj % 2][:, :],
                             [(wu[:, 1, c, ffc * 128:(ffc + 1) * 128],
                               G_fm[:, c, tg * 512:(tg + 1) * 512]) for c in range(NC)],
                             [(tM, 1)])
                wait("vector", tM, g)
                wait("vector", sM, silu_done[(tg * 4) + ffc // 2])
                umul[jj] = op("vector", lambda e, jj=jj, ffc=ffc, tg=tg:
                              e.tensor_tensor(act[:, ffc, tg * 512:(tg + 1) * 512],
                                              psB[jj % 2][:, :],
                                              act[:, ffc, tg * 512:(tg + 1) * 512],
                                              OP.mult), [(vM, 1)])
            uphase_last = tM.n
            prev_umul_last = umul[15]
            gtr_pend = []
            if e_ < NE - 1:
                gst = {"prev": gfm_freed.get((e_ + 1) % 2, 0)}
                gfm_freed[(e_ + 1) % 2] = prev_umul_last if e_ >= 1 else 0
                gst["prev"] = gfm_freed[(e_ + 1) % 2]
                for tt in range(8):
                    gtr_pend.append(lambda tt=tt, gst=gst, e2=e_ + 1, lc=gtm_loaded:
                                    gtr_tt(e2, tt, lc, gst))
                def fin_gtr(gst=gst, e2=e_ + 1):
                    gfm_ready[e2] = gst["prev"]
                gtr_pend.append(fin_gtr)
                if e_ < NE - 2:
                    wait("sync", tM, gphase_last)
                    wait("sync", dWgt, dWgt.n)
                    gtm_loaded = op("sync", lambda e, e_=e_:
                                    e.dma_start(out=Gtm[:, e_ % 2, :, :],
                                                in_=gbuf_d[(e_ + 2) * 1024:(e_ + 3) * 1024, :]
                                                .rearrange("(t p) f -> p t f", p=128)), [(dWgt, 16)])
            wait("gpsimd", tM, uphase_last)
            if e_ < NE - 1:
                wait("gpsimd", dWuu, dWuu.n)
                wuu_loaded = op("gpsimd", lambda e, e_=e_:
                                e.dma_start(out=wu[:, 1, :, :],
                                            in_=wup_r[:, e_ + 1, :, 0:1024]), [(dWuu, 16)])
            dmul = {}
            ddma = {}
            for jj in range(16):
                tt, og = jj // 2, jj % 2
                if jj == 0:
                    wait("tensor", dWdd, wd_loaded)
                    wait("tensor", vM, umul[15])
                    wait("tensor", sM, silu_done[7])
                if jj >= 4:
                    wait("tensor", vM, dmul[jj - 4])
                g = mm_group(psa_slot(jj)[:, :] if False else psa_slot(jj),
                             [(act[:, c, tt * 128:(tt + 1) * 128],
                               wd[:, c, og * 512:(og + 1) * 512]) for c in range(NC)],
                             [(tM, 1)])
                if gtr_pend and jj % 2 == 1:
                    gtr_pend.pop(0)()
                wait("vector", tM, g)
                if jj >= 2:
                    wait("vector", dDN[jj % 2], ddma[jj - 2])
                dmul[jj] = op("vector", lambda e, jj=jj:
                              e.tensor_copy(tmpbf[:, jj % 2, :], psa_slot(jj)), [(vM, 1)])
                wait("scalar", vM, dmul[jj])
                ddma[jj] = op("scalar", lambda e, e_=e_, tt=tt, og=og, jj=jj:
                              e.dma_start(out=dnbuf_d[e_ * 1024 + tt * 128:e_ * 1024 + (tt + 1) * 128,
                                                      og * 512:(og + 1) * 512],
                                          in_=tmpbf[:, jj % 2, :]), [(dDN[jj % 2], 16)])
            while gtr_pend:
                gtr_pend.pop(0)()
            prev_down_t = tM.n
            prev_dmul_last = vM.n
            if e_ < NE - 1:
                wait("gpsimd", tM, prev_down_t)
                wait("gpsimd", dWdd, dWdd.n)
                wd_loaded = op("gpsimd", lambda e, e_=e_:
                               e.dma_start(out=wd[:, :, :], in_=wdn_r[:, e_ + 1, :, :]), [(dWdd, 16)])
        dnbuf_all = (dDN[0].n, dDN[1].n)

        wait("gpsimd", dDN[0], dnbuf_all[0])
        wait("gpsimd", dDN[1], dnbuf_all[1])
        wait("sync", tM, tM.n)
        for eng in ("sync", "gpsimd"):
            wait(eng, dOA[0], dOA[0].n)
            wait(eng, dOA[1], dOA[1].n)
            wait(eng, dOX[0], dOX[0].n)
            wait(eng, dOX[1], dOX[1].n)
        gdma = {}
        gsum = {}
        gld1 = {}
        out_wr = {}
        ldo = {}
        for s in range(SPC):
            for t in range(NTT):
                it = s * NTT + t
                if it >= 2:
                    wait("sync", vM, gld1[it - 2])
                ldo[it] = op("sync", lambda e, s=s, t=t, it=it:
                             e.dma_start(out=tmpo[:, it % 2, :], in_=outr[:, s, t, :]),
                             [(dOL[it % 2], 16)])
                if it >= 2:
                    wait("gpsimd", vM, gsum[it - 2])
                last_g = 0
                for r in range(4):
                    last_g = op("gpsimd", lambda e, s=s, t=t, r=r, it=it:
                                e.indirect_dma_start(
                                    out=gth[:, it % 2, r, :], out_offset=None, in_=dnbuf_d,
                                    in_offset=IndirectOffsetOnAxis(selint[:, s, t, r:r + 1], 0)),
                                [(dGT[it % 2], 16)])
                gdma[it] = last_g
                wait("vector", dGT[it % 2], gdma[it])
                wait("vector", dOL[it % 2], ldo[it])
                if it >= 2:
                    wait("vector", dOG[it % 2], out_wr[it - 2])
                v0 = op("vector", lambda e, s=s, t=t, it=it:
                        e.scalar_tensor_tensor(tmpgs[:, it % 2, :], gth[:, it % 2, 0, :],
                                               gwsave[:, s, t, 0:1],
                                               tmpo[:, it % 2, :], OP.mult, OP.add), [(vM, 1)])
                gld1[it] = v0
                for r in range(1, 4):
                    wait("vector", vM, v0)
                    v0 = op("vector", lambda e, s=s, t=t, r=r, it=it:
                            e.scalar_tensor_tensor(tmpgs[:, it % 2, :], gth[:, it % 2, r, :],
                                                   gwsave[:, s, t, r:r + 1],
                                                   tmpgs[:, it % 2, :], OP.mult, OP.add), [(vM, 1)])
                gsum[it] = v0
                wait("scalar", vM, gsum[it])
                out_wr[it] = op("scalar", lambda e, s=s, t=t, it=it:
                                e.dma_start(out=outr[:, s, t, :], in_=tmpgs[:, it % 2, :]),
                                [(dOG[it % 2], 16)])

        wait("gpsimd", dOG[0], dOG[0].n)
        wait("gpsimd", dOG[1], dOG[1].n)
        wait("gpsimd", dOX[0], dOX[0].n)
        wait("gpsimd", dOX[1], dOX[1].n)
        wait("gpsimd", dOA[0], dOA[0].n)
        wait("gpsimd", dOA[1], dOA[1].n)


        with nc.Block() as block:
            @block.sync
            def _(e):
                for fn in P.q["sync"]:
                    fn(e)

            @block.tensor
            def _(e):
                for fn in P.q["tensor"]:
                    fn(e)

            @block.vector
            def _(e):
                for fn in P.q["vector"]:
                    fn(e)

            @block.scalar
            def _(e):
                for fn in P.q["scalar"]:
                    fn(e)

            @block.gpsimd
            def _(e):
                for fn in P.q["gpsimd"]:
                    fn(e)

    return nc



def _rope_tables():
    pos = np.arange(T, dtype=np.float64) / 32.0
    inv = 1.0 / (150000.0 ** (np.arange(0, DH, 2, dtype=np.float64) / DH))
    fr = pos[None, :] * inv[:, None]
    c = np.cos(fr)
    sn = np.sin(fr)
    cosq = np.tile(c, (4, 1))
    sinq = np.concatenate([-sn, sn, -sn, sn], axis=0)
    return (cosq.astype(np.float32).astype(ml_dtypes.bfloat16),
            sinq.astype(np.float32).astype(ml_dtypes.bfloat16))


def _masks():
    ki = np.arange(128)[:, None]
    qi = np.arange(512)[None, :]
    m = []
    for j in (-1, 0, 1, 2, 3):
        d = qi - ki - 128 * j
        m.append(((d >= 0) & (d <= 128)).astype(np.float32))
    return np.stack(m, 0).astype(ml_dtypes.bfloat16)


def _perm_evod():
    return np.concatenate([np.arange(0, DH, 2), np.arange(1, DH, 2)])


def kernel(**inputs):
    x = np.ascontiguousarray(np.asarray(inputs["L0_attn_x"], dtype=np.float32))
    wqkva = np.ascontiguousarray(np.asarray(inputs["L0_attn_w_qkv"], dtype=np.float32))
    wouta = np.ascontiguousarray(np.asarray(inputs["L0_attn_w_out"], dtype=np.float32))
    anw = np.ascontiguousarray(np.asarray(inputs["L0_ffn_attn_norm_w"], dtype=np.float32))
    fnw = np.ascontiguousarray(np.asarray(inputs["L0_ffn_ffn_norm_w"], dtype=np.float32))
    qkvw = np.asarray(inputs["L0_ffn_qkv_w"], dtype=np.float32)
    qkvb = np.asarray(inputs["L0_ffn_qkv_b"], dtype=np.float32)
    projw = np.asarray(inputs["L0_ffn_proj_w"], dtype=np.float32)
    projb = np.ascontiguousarray(np.asarray(inputs["L0_ffn_proj_b"], dtype=np.float32)[None, :])
    sinkv = np.ascontiguousarray(np.asarray(inputs["L0_ffn_attn_sink"], dtype=np.float32)[None, :])
    gatew = np.ascontiguousarray(np.asarray(inputs["L0_ffn_gate_w"], dtype=np.float32))
    gateb = np.ascontiguousarray(np.asarray(inputs["L0_ffn_gate_b"], dtype=np.float32)[None, :])
    wup = np.ascontiguousarray(np.asarray(inputs["L0_ffn_W_up"], dtype=np.float32))
    wdn = np.ascontiguousarray(np.asarray(inputs["L0_ffn_W_down"], dtype=np.float32))

    ev = _perm_evod()
    qperm = np.concatenate([HEAD_ORDER[i] * DH + ev for i in range(HF)])
    kperm = np.concatenate([1024 + kv * DH + ev for kv in range(KVH)])
    vperm = np.arange(1280, 1536)
    colperm = np.concatenate([qperm, kperm, vperm])
    qkvw_p = np.ascontiguousarray(qkvw[:, colperm])
    qkvb_p = np.ascontiguousarray(qkvb[colperm][None, :])
    rowperm = np.concatenate([HEAD_ORDER[i] * DH + np.arange(DH) for i in range(HF)])
    projw_p = np.ascontiguousarray(projw[rowperm, :])

    ident = np.eye(128, dtype=np.float32).astype(ml_dtypes.bfloat16)
    cosq, sinq = _rope_tables()
    maskf = _masks()
    lt16 = np.triu(np.ones((16, 16), np.float32), 1).astype(ml_dtypes.bfloat16)
    lt128 = np.triu(np.ones((128, 128), np.float32), 1).astype(ml_dtypes.bfloat16)
    eoff = np.zeros((1, 64), np.float32)
    for s_ in range(SPC):
        eoff[0, 32 * s_:32 * s_ + 32] = np.arange(NE) * 1024 + 512 * s_
    eoff = eoff.astype(ml_dtypes.bfloat16)

    common = dict(wqkva=wqkva, wouta=wouta, anw=anw, fnw=fnw,
                  qkvwf=qkvw_p, qkvbf=qkvb_p, projw=projw_p, projb=projb,
                  sink=sinkv, gatew=gatew, gateb=gateb, wup=wup, wdn=wdn,
                  ident=ident, cosq=np.ascontiguousarray(cosq),
                  sinq=np.ascontiguousarray(sinq),
                  maskf=np.ascontiguousarray(maskf),
                  lt16=np.ascontiguousarray(lt16), lt128=np.ascontiguousarray(lt128),
                  eoff=np.ascontiguousarray(eoff))
    in_maps = [dict(common, x=np.ascontiguousarray(x[2 * c:2 * c + 2])) for c in range(NCORES)]

    ncprog = build_program()
    res = run_bass_kernel_spmd(ncprog, in_maps, core_ids=list(range(NCORES)))
    out = np.concatenate([np.asarray(res.results[c]["out"]) for c in range(NCORES)], axis=0)
    return out.astype(np.float32)


# version 21
# speedup vs baseline: 1.0649x; 1.0649x over previous
import numpy as np
import ml_dtypes
import concourse.bass as bass
import concourse.mybir as mybir
from concourse.bass import IndirectOffsetOnAxis
from concourse.bass_utils import run_bass_kernel_spmd
from contextlib import ExitStack

F32 = mybir.dt.float32
BF16 = mybir.dt.bfloat16
AF = mybir.ActivationFunctionType
AX = mybir.AxisListType
OP = mybir.AluOpType

NCORES = 8
SPC = 2
T = 2048
D = 1024
NC = 8
NTT = 16
NQG = 4
HA = 16
HF = 16
KVH = 4
DH = 64
NE = 32
DFF = 1024

HEAD_ORDER = [0, 4, 1, 5, 2, 6, 3, 7, 8, 12, 9, 13, 10, 14, 11, 15]

DEBUG = False

ENGINES = ("sync", "tensor", "vector", "scalar", "gpsimd")


class Sem:
    __slots__ = ("h", "n")

    def __init__(self, h):
        self.h = h
        self.n = 0


class Plan:
    def __init__(self):
        self.q = {e: [] for e in ENGINES}

    def op(self, eng, fn, incs=()):
        def run(e, fn=fn, incs=tuple(incs)):
            ins = fn(e)
            for s, k in incs:
                ins.then_inc(s.h, k)
        self.q[eng].append(run)
        last = 0
        for s, k in incs:
            s.n += k
            last = s.n
        return last

    def wait(self, eng, sem, n=None):
        v = sem.n if n is None else n
        if v is None or v <= 0:
            return
        self.q[eng].append(lambda e, h=sem.h, v=v: e.wait_ge(h, v))


def build_program():
    nc = bass.Bass("TRN2", target_bir_lowering=False)

    x_d = nc.dram_tensor("x", [SPC, T, D], F32, kind="ExternalInput").ap()
    wqkva_d = nc.dram_tensor("wqkva", [D, 3 * D], F32, kind="ExternalInput").ap()
    wouta_d = nc.dram_tensor("wouta", [D, D], F32, kind="ExternalInput").ap()
    anw_d = nc.dram_tensor("anw", [D], F32, kind="ExternalInput").ap()
    fnw_d = nc.dram_tensor("fnw", [D], F32, kind="ExternalInput").ap()
    qkvwf_d = nc.dram_tensor("qkvwf", [D, 1536], F32, kind="ExternalInput").ap()
    qkvbf_d = nc.dram_tensor("qkvbf", [1, 1536], F32, kind="ExternalInput").ap()
    projw_d = nc.dram_tensor("projw", [D, D], F32, kind="ExternalInput").ap()
    projb_d = nc.dram_tensor("projb", [1, D], F32, kind="ExternalInput").ap()
    sink_d = nc.dram_tensor("sink", [1, HF], F32, kind="ExternalInput").ap()
    gatew_d = nc.dram_tensor("gatew", [D, NE], F32, kind="ExternalInput").ap()
    gateb_d = nc.dram_tensor("gateb", [1, NE], F32, kind="ExternalInput").ap()
    wup_d = nc.dram_tensor("wup", [NE, D, 2 * DFF], F32, kind="ExternalInput").ap()
    wdn_d = nc.dram_tensor("wdn", [NE, DFF, D], F32, kind="ExternalInput").ap()
    ident_d = nc.dram_tensor("ident", [128, 128], BF16, kind="ExternalInput").ap()
    cosq_d = nc.dram_tensor("cosq", [128, T], BF16, kind="ExternalInput").ap()
    sinq_d = nc.dram_tensor("sinq", [128, T], BF16, kind="ExternalInput").ap()
    maskf_d = nc.dram_tensor("maskf", [5, 128, 512], BF16, kind="ExternalInput").ap()
    lt16_d = nc.dram_tensor("lt16", [16, 16], BF16, kind="ExternalInput").ap()
    lt128_d = nc.dram_tensor("lt128", [128, 128], BF16, kind="ExternalInput").ap()
    eoff_d = nc.dram_tensor("eoff", [1, 64], BF16, kind="ExternalInput").ap()
    gbuf_d = nc.dram_tensor("gbuf", [NE * 1024, D], BF16).ap()
    dnbuf_d = nc.dram_tensor("dnbuf", [NE * 1024, D], BF16).ap()
    cnt_d = nc.dram_tensor("cnt_d", [NE, NTT], F32).ap()
    base_d = nc.dram_tensor("base_d", [NTT, NE], F32).ap()
    out_d = nc.dram_tensor("out", [SPC, T, D], F32, kind="ExternalOutput").ap()
    if DEBUG:
        dbg_stats = nc.dram_tensor("dbg_stats", [128, NTT, 4], F32, kind="ExternalOutput").ap()
        dbg_xn = nc.dram_tensor("dbg_xn", [128, NC, T], F32, kind="ExternalOutput").ap()
        dbg_ka = nc.dram_tensor("dbg_ka", [128, NC, T], F32, kind="ExternalOutput").ap()
        dbg_qa = nc.dram_tensor("dbg_qa", [128, NC, 512], F32, kind="ExternalOutput").ap()
        dbg_va = nc.dram_tensor("dbg_va", [128, NTT, HA * 65], F32, kind="ExternalOutput").ap()
        dbg_p = nc.dram_tensor("dbg_p", [128, 2, 1024], F32, kind="ExternalOutput").ap()
        dbg_atno = nc.dram_tensor("dbg_atno", [128, NC, 512], F32, kind="ExternalOutput").ap()
        dbg_hb = nc.dram_tensor("dbg_hb", [128, NC, T], F32, kind="ExternalOutput").ap()
        dbg_kf = nc.dram_tensor("dbg_kf", [128, 2, T], F32, kind="ExternalOutput").ap()
        dbg_qf = nc.dram_tensor("dbg_qf", [128, NC, 512], F32, kind="ExternalOutput").ap()
        dbg_yf = nc.dram_tensor("dbg_yf", [128, NC, 512], F32, kind="ExternalOutput").ap()
        dbg_xf = nc.dram_tensor("dbg_xf", [128, NC, T], F32, kind="ExternalOutput").ap()
        dbg_rstdh2 = nc.dram_tensor("dbg_rstdh2", [1, T], F32, kind="ExternalOutput").ap()
        dbg_h2 = nc.dram_tensor("dbg_h2", [128, NC, T], F32, kind="ExternalOutput").ap()
        dbg_sel = nc.dram_tensor("dbg_sel", [128, SPC, NTT, 4], F32, kind="ExternalOutput").ap()
        dbg_gw = nc.dram_tensor("dbg_gw", [128, SPC, NTT, 4], F32, kind="ExternalOutput").ap()
        dbg_base = nc.dram_tensor("dbg_base", [1, 512], F32, kind="ExternalOutput").ap()
        dbg_gth = nc.dram_tensor("dbg_gth", [128, 4, 1024], F32, kind="ExternalOutput").ap()
        dbg_gbuf = nc.dram_tensor("dbg_gbuf", [256, 1024], F32, kind="ExternalOutput").ap()
        dbg_geq = nc.dram_tensor("dbg_geq", [128, NTT, 4, NE], F32, kind="ExternalOutput").ap()
        dbg_bf = nc.dram_tensor("dbg_bf", [128, 512], F32, kind="ExternalOutput").ap()
        dbg_sfl = nc.dram_tensor("dbg_sfl", [128, NTT, 4], F32, kind="ExternalOutput").ap()


    xr = x_d.rearrange("s (t p) f -> p s t f", p=128)
    outr = out_d.rearrange("s (t p) f -> p s t f", p=128)
    wqkva_r = wqkva_d.rearrange("(c p) n -> p c n", p=128)
    wouta_r = wouta_d.rearrange("(c p) n -> p c n", p=128)
    qkvwf_r = qkvwf_d.rearrange("(c p) n -> p c n", p=128)
    projw_r = projw_d.rearrange("(c p) n -> p c n", p=128)
    gatew_r = gatew_d.rearrange("(c p) n -> p c n", p=128)
    anw_r = anw_d.rearrange("(c p) -> p c", p=128)
    fnw_r = fnw_d.rearrange("(c p) -> p c", p=128)
    wup_r = wup_d.rearrange("e (c p) n -> p e c n", p=128)
    wdn_r = wdn_d.rearrange("e (c p) n -> p e c n", p=128)

    P = Plan()
    op, wait = P.op, P.wait
    es = ExitStack()
    with es:
        def sbuf(name, shape, dt):
            return es.enter_context(nc.sbuf_tensor(name, shape, dt))

        ident = sbuf("ident_s", [128, 128], BF16)
        xn = sbuf("xn_s", [128, NC, T], BF16)
        stats = sbuf("stats_s", [128, NTT, 4], F32)
        onesb = sbuf("onesb", [1, 512], BF16)
        onescol = sbuf("onescol", [128, 1], BF16)
        onesf = sbuf("onesf", [1, 128], F32)
        anw = sbuf("anw_s", [128, NC], F32)
        fnw = sbuf("fnw_s", [128, NC], F32)
        gatew = sbuf("gatew_s", [128, NC, NE], BF16)
        gateb = sbuf("gateb_s", [1, NE], BF16)
        sink = sbuf("sink_s", [1, HF], F32)
        sinkexp = sbuf("sinkexp", [1, HF], F32)
        rstdh2 = sbuf("rstdh2", [1, T], BF16)
        hrow = sbuf("hrow", [1, 512], BF16)
        hrow2 = sbuf("hrow2", [1, 2, 512], BF16)
        rstrow = sbuf("rstrow", [1, 2, 512], F32)
        rowtmp = sbuf("rowtmp", [1, 2, 512], F32)
        epsb = sbuf("epsb", [128, 1], F32)
        lt16b = sbuf("lt16b", [16, 16], BF16)
        lt128b = sbuf("lt128b", [128, 128], BF16)
        eoffb = sbuf("eoffb", [1, 64], BF16)
        selint = sbuf("selint", [128, SPC, NTT, 4], mybir.dt.int32)
        gwsave = sbuf("gwsave", [128, SPC, NTT, 4], F32)

        AKB = 140
        arena = sbuf("arena", [128, AKB * 1024 // 4], F32)
        a_bf = arena.bitcast(BF16).ap()
        a_f32 = arena.ap()

        def carve(bf, off_kb, shape):
            base = a_bf if bf else a_f32
            isz = 2 if bf else 4
            start = off_kb * 1024 // isz
            n = int(np.prod(shape))
            ap = base[:, start:start + n]
            if len(shape) == 1:
                return ap
            pat = {2: "p (a b) -> p a b", 3: "p (a b c) -> p a b c"}[len(shape)]
            return ap.rearrange(pat, **dict(zip("abc", shape)))

        xbuf = carve(False, 32, [2, 1024])
        xntm = carve(True, 40, [1024])
        wbuf2 = carve(True, 0, [NC, 2048])
        Qa = carve(True, 32, [NC, 512])
        Ka = carve(True, 40, [NC, T])
        Va = carve(True, 72, [NTT, HA * 65])
        atno = carve(True, 106, [2, NC, 512])
        Pb2 = carve(True, 122, [4, 1024])
        tmp2 = carve(False, 130, [4, 512])
        wqkvf = carve(True, 0, [NC, 1024])
        wproj = carve(True, 16, [NC, 1024])
        hb = carve(True, 32, [NC, T])
        Qf = carve(True, 64, [NC, 512])
        Kf = carve(True, 72, [2, T])
        Vf = carve(True, 80, [NTT, KVH * 65])
        yf = carve(True, 89, [NC, 512])
        xnsq = carve(True, 97, [2, 512])
        swp = carve(True, 99, [2, 1024])
        cosb = carve(True, 103, [T])
        sinb = carve(True, 107, [T])
        maskb = carve(True, 111, [5, 512])
        xbuf3 = carve(False, 116, [1024])
        qkvbrow = carve(True, 120, [1536])
        projbrow = carve(True, 123, [1024])
        Pb3 = carve(True, 125, [4, 512])
        tmp3 = carve(False, 129, [2, 1024])
        h2 = carve(True, 0, [NC, T])
        wu = carve(True, 32, [2, NC, 1024])
        wd = carve(True, 64, [NC, 1024])
        G_fm2 = carve(True, 80, [NC, 1024])
        gwork = carve(False, 96, [NTT, NE])
        geq = carve(False, 98, [NTT, 4, NE])
        gm = carve(False, 106, [NTT, 8])
        indb = carve(True, 107, [NTT, NE])
        sfl = carve(False, 108, [NTT, 4])
        sfield = carve(False, 109, [NE])
        cntsb = carve(False, 110, [16])
        cntC = carve(True, 110, [64])
        basesb = carve(False, 111, [NE])
        baserow = carve(False, 112, [512])
        bfield = carve(False, 114, [512])
        h2tm = carve(True, 116, [2, 1024])
        Gtm = carve(True, 0, [2, NC, 1024])
        G_fm1 = carve(True, 96, [NC, 1024])
        act = carve(True, 112, [NC, 1024])
        tmpbf = carve(True, 128, [2, 512])
        gth = carve(True, 96, [2, 4, 1024])
        tmpgs = carve(False, 112, [2, 1024])
        tmpo = carve(False, 0, [2, 1024])
        xfsq = carve(True, 138, [2, 512])

        psA = [es.enter_context(nc.psum_tensor(f"psA{i}", [128, 1024], F32)) for i in range(2)]
        psB = [es.enter_context(nc.psum_tensor(f"psB{i}", [128, 512], F32)) for i in range(2)]
        psC = es.enter_context(nc.psum_tensor("psC", [128, 512], F32))
        psT = es.enter_context(nc.psum_tensor("psT", [128, 1024], BF16))

        dX = Sem(es.enter_context(nc.semaphore("dX")))
        dW = Sem(es.enter_context(nc.semaphore("dW")))
        dO = Sem(es.enter_context(nc.semaphore("dO")))
        tM = Sem(es.enter_context(nc.semaphore("tM")))
        vM = Sem(es.enter_context(nc.semaphore("vM")))
        sM = Sem(es.enter_context(nc.semaphore("sM")))
        tP = Sem(es.enter_context(nc.semaphore("tP")))
        dC = Sem(es.enter_context(nc.semaphore("dC")))
        dXp = [Sem(es.enter_context(nc.semaphore(f"dXp{i}"))) for i in range(2)]
        dX3 = Sem(es.enter_context(nc.semaphore("dX3")))
        dWug = Sem(es.enter_context(nc.semaphore("dWug")))
        dWuu = Sem(es.enter_context(nc.semaphore("dWuu")))
        dWdd = Sem(es.enter_context(nc.semaphore("dWdd")))
        dWgt = Sem(es.enter_context(nc.semaphore("dWgt")))
        dSC = [Sem(es.enter_context(nc.semaphore(f"dSC{i}"))) for i in range(2)]
        dDN = [Sem(es.enter_context(nc.semaphore(f"dDN{i}"))) for i in range(2)]
        dGT = [Sem(es.enter_context(nc.semaphore(f"dGT{i}"))) for i in range(2)]
        dOA = [Sem(es.enter_context(nc.semaphore(f"dOA{i}"))) for i in range(2)]
        dOX = [Sem(es.enter_context(nc.semaphore(f"dOX{i}"))) for i in range(2)]
        dOG = [Sem(es.enter_context(nc.semaphore(f"dOG{i}"))) for i in range(2)]
        dOL = [Sem(es.enter_context(nc.semaphore(f"dOL{i}"))) for i in range(2)]
        psc_free = [0]

        def mm(ps_ap, lhsT, rhs, st, sp, incs=(), tile_position=None):
            return op("tensor",
                      lambda e, a=ps_ap, l=lhsT, r=rhs, st=st, sp=sp, tp_=tile_position:
                          e.matmul(a, l, r, start=st, stop=sp, tile_position=tp_), incs)

        def mm_group(ps_ap, pairs, incs=()):
            n = len(pairs)
            last = 0
            for i, (lf, rf) in enumerate(pairs):
                last = mm(ps_ap, lf, rf, i == 0, i == n - 1,
                          incs if i == n - 1 else ())
            return last

        def act3(fp, qg):
            return act[:, 2 * fp:2 * fp + 2, qg * 512:(qg + 1) * 512]

        c_w2kv = op("gpsimd", lambda e:
                    e.dma_start(out=wbuf2[:, :, :], in_=wqkva_r[:, :, 1024:3072]), [(dW, 16)])
        c_ident = op("sync", lambda e: e.dma_start(out=ident[:, :], in_=ident_d), [(dC, 16)])
        c_sink = op("sync", lambda e: e.dma_start(out=sink[0:1, :], in_=sink_d), [(dC, 16)])
        op("gpsimd", lambda e: e.dma_start(out=anw[:, :], in_=anw_r, allow_slow_non_contiguous=True), [(dW, 16)])
        op("gpsimd", lambda e: e.dma_start(out=fnw[:, :], in_=fnw_r, allow_slow_non_contiguous=True), [(dW, 16)])
        op("gpsimd", lambda e: e.dma_start(out=gatew[:, :, :], in_=gatew_r), [(dW, 16)])
        op("gpsimd", lambda e: e.dma_start(out=gateb[0:1, :], in_=gateb_d), [(dW, 16)])
        op("vector", lambda e: e.memset(onesb[0:1, :], 1.0), [(vM, 1)])
        op("vector", lambda e: e.memset(onescol[:, :], 1.0), [(vM, 1)])
        c_ones = op("vector", lambda e: e.memset(onesf[0:1, :], 1.0), [(vM, 1)])
        c_eps = op("vector", lambda e: e.memset(epsb[:, :], 1e-5), [(vM, 1)])
        op("sync", lambda e: e.dma_start(out=lt16b[:, :], in_=lt16_d), [(dC, 16)])
        op("sync", lambda e: e.dma_start(out=lt128b[:, :], in_=lt128_d), [(dC, 16)])
        c_lt = op("sync", lambda e: e.dma_start(out=eoffb[0:1, :], in_=eoff_d), [(dC, 16)])
        wait("scalar", dC, dC.n)
        op("scalar", lambda e: e.activation(sinkexp[0:1, :], sink[0:1, :], AF.Exp), [(sM, 1)])

        def psa_slot(j):
            return psA[(j // 2) % 2][:, (j % 2) * 512:(j % 2 + 1) * 512]

        seq_scatter_done = {}
        for s in range(SPC):
            wait("sync", tM, tM.n)
            wait("sync", vM, vM.n)
            wait("sync", sM, sM.n)
            wait("sync", dX, dX.n)
            wait("sync", dC, dC.n)
            wait("sync", dXp[0], dXp[0].n)
            wait("sync", dXp[1], dXp[1].n)
            wait("sync", dX3, dX3.n)
            if s >= 1:
                wait("gpsimd", tM, tM.n)
                wait("gpsimd", vM, vM.n)
                wait("gpsimd", sM, sM.n)
                c_w2kv = op("gpsimd", lambda e:
                            e.dma_start(out=wbuf2[:, :, :], in_=wqkva_r[:, :, 1024:3072]), [(dW, 16)])
            ld = [0] * NTT
            p1v = [0] * NTT
            p1t = [0] * NTT
            p1e = [0] * NTT
            sq = [0] * NTT
            r1 = [0] * NTT
            for t in range(NTT):
                b = t % 2
                if t >= 2:
                    wait("sync", vM, p1v[t - 2])
                    wait("sync", dXp[b], dXp[b].n)
                ld[t] = op("sync", lambda e, s=s, t=t, b=b:
                           e.dma_start(out=xbuf[:, b, :], in_=xr[:, s, t, :]), [(dXp[b], 16)])
                wait("scalar", dXp[b], ld[t])
                if t == 0:
                    wait("scalar", tM, tM.n)
                if t >= 2:
                    wait("scalar", vM, r1[t - 2])
                sq[t] = op("scalar", lambda e, b=b:
                           e.activation(psA[b][:, :], xbuf[:, b, :], AF.Square), [(sM, 1)])
                wait("vector", dXp[b], ld[t])
                op("vector", lambda e, b=b, t=t:
                   e.reduce_sum(stats[:, t, 0:1], xbuf[:, b, :], axis=AX.X), [(vM, 1)])
                wait("vector", sM, sq[t])
                r1[t] = op("vector", lambda e, b=b, t=t:
                           e.reduce_sum(stats[:, t, 1:2], psA[b][:, :], axis=AX.X), [(vM, 1)])
                a1 = op("vector", lambda e, t=t:
                        e.tensor_scalar_mul(stats[:, t, 2:3], stats[:, t, 0:1], -1.0 / D), [(vM, 1)])
                wait("vector", vM, a1)
                a2 = op("vector", lambda e, t=t:
                        e.tensor_tensor(stats[:, t, 3:4], stats[:, t, 2:3], stats[:, t, 2:3], OP.mult), [(vM, 1)])
                a3 = op("vector", lambda e, t=t:
                        e.tensor_scalar_mul(stats[:, t, 1:2], stats[:, t, 1:2], 1.0 / D), [(vM, 1)])
                wait("vector", vM, a3)
                a4 = op("vector", lambda e, t=t:
                        e.scalar_tensor_tensor(stats[:, t, 1:2], stats[:, t, 3:4], -1.0,
                                               stats[:, t, 1:2], OP.mult, OP.add), [(vM, 1)])
                wait("scalar", vM, a4)
                b1 = op("scalar", lambda e, t=t:
                        e.activation(stats[:, t, 3:4], stats[:, t, 1:2], AF.Ln, bias=epsb[:, 0:1]), [(sM, 1)])
                wait("scalar", sM, b1)
                b2 = op("scalar", lambda e, t=t:
                        e.activation(stats[:, t, 3:4], stats[:, t, 3:4], AF.Exp, scale=-0.5), [(sM, 1)])
                wait("vector", sM, b2)
                if t >= 1:
                    wait("vector", tM, p1t[t - 1])
                p1v[t] = op("vector", lambda e, b=b, t=t:
                            e.tensor_scalar(xntm[:, :], xbuf[:, b, :], stats[:, t, 2:3],
                                            stats[:, t, 3:4], OP.add, OP.mult), [(vM, 1)])
                wait("tensor", vM, p1v[t])
                if t == 0 and s == 0:
                    wait("tensor", dC, c_ident)
                if t >= 1:
                    wait("tensor", vM, p1e[t - 1])
                for c in range(NC):
                    p1t[t] = op("tensor", lambda e, c=c:
                                e.transpose(psT[:, c * 128:(c + 1) * 128],
                                            xntm[:, c * 128:(c + 1) * 128], ident[:, :]),
                                [(tM, 1)] if c == NC - 1 else ())
                wait("vector", tM, p1t[t])
                p1e[t] = op("vector", lambda e, t=t:
                            e.tensor_copy(xn[:, :, t * 128:(t + 1) * 128],
                                          psT[:, :].rearrange("p (a b) -> p a b", a=NC, b=128)),
                            [(vM, 1)])

            if DEBUG and s == 0:
                wait("sync", vM, vM.n)
                op("sync", lambda e: e.dma_start(out=dbg_stats, in_=stats[:, :, :]), [(dX, 16)])
                wait("gpsimd", vM, vM.n)
                op("gpsimd", lambda e: e.dma_start(out=dbg_xn, in_=xn[:, :, :]), [(dX, 16)])

            evacA = {}
            jk = 0
            for qg in range(NQG):
                for kc in range(NC):
                    sl = psa_slot(jk)
                    wait("tensor", dW, c_w2kv)
                    if jk >= 4:
                        wait("tensor", vM, evacA[jk - 4])
                    g = mm_group(sl, [(wbuf2[:, c, kc * 128:(kc + 1) * 128],
                                       xn[:, c, qg * 512:(qg + 1) * 512]) for c in range(NC)],
                                 [(tM, 1)])
                    wait("vector", tM, g)
                    evacA[jk] = op("vector", lambda e, sl=sl, kc=kc, qg=qg:
                                   e.tensor_copy(Ka[:, kc, qg * 512:(qg + 1) * 512], sl), [(vM, 1)])
                    jk += 1
            op("vector", lambda e: e.memset(
                Va[:, :, :].rearrange("p t (h x) -> p t h x", x=65)[:, :, :, 64:65], 1.0), [(vM, 1)])
            jb = 0
            evacB = {}
            for tt in range(NTT):
                for vb in range(2):
                    pb = psB[jb % 2]
                    if jb >= 2:
                        wait("tensor", vM, evacB[jb - 2])
                    g = mm_group(pb[:, :], [(xn[:, c, tt * 128:(tt + 1) * 128],
                                             wbuf2[:, c, 1024 + vb * 512:1024 + (vb + 1) * 512])
                                            for c in range(NC)], [(tM, 1)])
                    wait("vector", tM, g)
                    dst = Va[:, tt, vb * 8 * 65:(vb * 8 + 8) * 65].rearrange(
                        "p (a b) -> p a b", a=8, b=65)[:, :, 0:64]
                    evacB[jb] = op("vector", lambda e, pb=pb, dst=dst:
                                   e.tensor_copy(dst, pb[:, :].rearrange("p (a b) -> p a b", a=8, b=64)),
                                   [(vM, 1)])
                    jb += 1
            wait("gpsimd", tM, tM.n)
            c_w2q = op("gpsimd", lambda e:
                       e.dma_start(out=wbuf2[:, :, 0:1024], in_=wqkva_r[:, :, 0:1024]), [(dW, 16)])
            op("gpsimd", lambda e:
               e.dma_start(out=wbuf2[:, :, 1024:2048], in_=wouta_r[:, :, :]), [(dW, 16)])
            c_w2o = dW.n

            evmulA = {}
            recipA = {}

            def denA_s1(ph):
                wait("scalar", tM, tM.n)
                wait("scalar", tP, tP.n)
                r0 = op("scalar", lambda e, ph=ph:
                        e.activation(rowtmp[0:1, ph % 2, :], psB[ph % 2][64:65, :], AF.Ln), [(sM, 1)])
                wait("scalar", sM, r0)
                recipA[ph] = op("scalar", lambda e, ph=ph:
                                e.activation(hrow2[0:1, ph % 2, :], rowtmp[0:1, ph % 2, :],
                                             AF.Exp, scale=-1.0), [(sM, 1)])

            def denA_s2(ph, qg):
                wait("tensor", sM, recipA[ph])
                wait("tensor", vM, psc_free[0])
                bc = mm(psC[0:64, :], onesb[0:1, 0:64], hrow2[0:1, ph % 2, :], True, True, [(tP, 1)])
                wait("vector", tP, bc)
                base, ch = 64 * (ph % 2), ph // 2
                d3 = op("vector", lambda e, base=base, ch=ch, ph=ph, qg=qg:
                        e.tensor_copy(atno[base:base + 64, qg % 2, ch, :], psB[ph % 2][0:64, :]), [(vM, 1)])
                wait("vector", vM, d3)
                evmulA[ph] = op("vector", lambda e, base=base, ch=ch, qg=qg:
                                e.tensor_tensor(atno[base:base + 64, qg % 2, ch, :],
                                                atno[base:base + 64, qg % 2, ch, :],
                                                psC[0:64, :], OP.mult), [(vM, 1)])
                psc_free[0] = evmulA[ph]

            gi = 0
            exp_done = {}
            pv_done = {}
            jA_ = [0]
            Acopy = {}
            dmaA = {}
            pend2 = []
            for qg in range(NQG):
                for qc in range(NC):
                    sl = psa_slot(jk)
                    wait("tensor", dW, c_w2q)
                    if jk >= 4:
                        wait("tensor", vM, evacA[jk - 4])
                    g = mm_group(sl, [(wbuf2[:, c, qc * 128:(qc + 1) * 128],
                                       xn[:, c, qg * 512:(qg + 1) * 512]) for c in range(NC)],
                                 [(tM, 1)])
                    wait("vector", tM, g)
                    evacA[jk] = op("vector", lambda e, sl=sl, qc=qc:
                                   e.tensor_copy(Qa[:, qc, :], sl), [(vM, 1)])
                    jk += 1
                if DEBUG and s == 0 and qg == 0:
                    wait("gpsimd", vM, vM.n)
                    op("gpsimd", lambda e: e.dma_start(out=dbg_ka, in_=Ka[:, :, :]), [(dX, 16)])
                    op("gpsimd", lambda e: e.dma_start(out=dbg_qa, in_=Qa[:, :, :]), [(dX, 16)])
                    op("gpsimd", lambda e: e.dma_start(out=dbg_va, in_=Va[:, :, :]), [(dX, 16)])
                NPAIR = NTT // 2
                items = [(h, kp) for h in range(HA) for kp in range(NPAIR)]

                def emit_pv(idx2):
                    h2_, kp2 = items[idx2]
                    gi2 = gi0 + idx2
                    wait("tensor", sM, exp_done[gi2])
                    if kp2 == 0 and h2_ >= 2:
                        wait("tensor", vM, evmulA[h2_ - 2])
                    last = 0
                    for half in range(2):
                        kt = 2 * kp2 + half
                        last = mm(psB[h2_ % 2][0:65, :], Va[:, kt, 65 * h2_:65 * h2_ + 65],
                                  Pb2[:, gi2 % 2, half * 512:(half + 1) * 512],
                                  (kp2 == 0 and half == 0), (kp2 == NPAIR - 1 and half == 1),
                                  [(tM, 1)] if half == 1 else ())
                    pv_done[gi2] = last
                    if kp2 == NPAIR - 1:
                        denA_s1(h2_)
                        pendA.append((h2_, idx2))

                gi0 = gi
                pendA = []
                for idx, (h, kp) in enumerate(items):
                    base, ch = 64 * (h % 2), h // 2
                    if idx == 0:
                        wait("tensor", vM, vM.n)
                    if gi >= 2:
                        wait("tensor", sM, exp_done[gi - 2])
                    stp = 0
                    for half in range(2):
                        kt = 2 * kp + half
                        stp = mm(psA[gi % 2][:, half * 512:(half + 1) * 512],
                                 Ka[base:base + 64, ch, kt * 128:(kt + 1) * 128],
                                 Qa[base:base + 64, ch, :], True, True,
                                 [(tM, 1)] if half == 1 else ())
                    if pendA and idx - pendA[0][1] >= 2:
                        denA_s2(pendA.pop(0)[0], qg)
                    if idx >= 1:
                        emit_pv(idx - 1)
                    wait("scalar", tM, stp)
                    if gi >= 2:
                        wait("scalar", tM, pv_done[gi - 2])
                    exp_done[gi] = op("scalar", lambda e, gi=gi:
                                      e.activation(Pb2[:, gi % 2, :], psA[gi % 2][:, :],
                                                   AF.Exp, scale=1.0 / 32.0), [(sM, 1)])
                    if DEBUG and s == 0 and qg == 0 and gi == 0:
                        wait("gpsimd", sM, exp_done[0])
                        op("gpsimd", lambda e: e.dma_start(out=dbg_p, in_=Pb2[:, :, :]), [(dX, 16)])
                    gi += 1
                emit_pv(len(items) - 1)
                while pendA:
                    denA_s2(pendA.pop(0)[0], qg)
                if DEBUG and s == 0 and qg == 0:
                    wait("gpsimd", vM, vM.n)
                    op("gpsimd", lambda e: e.dma_start(out=dbg_atno, in_=atno[:, :, :]), [(dX, 16)])
                def mk_Amm(tt, og, qg=qg):
                    def emit():
                        jA = jA_[0]
                        wait("tensor", vM, evmulA[14])
                        wait("tensor", vM, evmulA[15])
                        wait("tensor", vM, psc_free[0])
                        wait("tensor", dW, c_w2o)
                        g = mm_group(psC[:, :],
                                     [(atno[:, qg % 2, c, (tt % 4) * 128:(tt % 4 + 1) * 128],
                                       wbuf2[:, c, 1024 + og * 512:1024 + (og + 1) * 512])
                                      for c in range(NC)], [(tM, 1)])
                        wait("vector", tM, g)
                        if jA >= 2:
                            wait("vector", dOA[jA % 2], dmaA[jA - 2])
                        Acopy[jA] = op("vector", lambda e, j=jA:
                                       e.tensor_copy(tmp2[:, j % 4, :], psC[:, :]), [(vM, 1)])
                        psc_free[0] = Acopy[jA]
                        wait("sync", vM, Acopy[jA])
                        dmaA[jA] = op("sync", lambda e, s=s, tt=tt, og=og, j=jA:
                                      e.dma_start(out=outr[:, s, tt, og * 512:(og + 1) * 512],
                                                  in_=tmp2[:, j % 4, :]), [(dOA[jA % 2], 16)])
                        jA_[0] = jA + 1
                    return emit

                for tt4 in range(4):
                    for og in range(2):
                        pend2.append(mk_Amm(qg * 4 + tt4, og))
            while pend2:
                pend2.pop(0)()
            a_writes_done = (dOA[0].n, dOA[1].n)

            wait("gpsimd", tM, tM.n)
            wait("gpsimd", vM, vM.n)
            wait("gpsimd", sM, sM.n)
            c_wfkv = op("gpsimd", lambda e:
                        e.dma_start(out=wqkvf[:, :, 0:512], in_=qkvwf_r[:, :, 1024:1536]), [(dW, 16)])
            c_wproj = op("gpsimd", lambda e:
                         e.dma_start(out=wproj[:, :, :], in_=projw_r[:, :, :]), [(dW, 16)])
            op("gpsimd", lambda e: e.dma_start(out=qkvbrow[0:1, :], in_=qkvbf_d), [(dW, 16)])
            c_brow = op("gpsimd", lambda e: e.dma_start(out=projbrow[0:1, :], in_=projb_d), [(dW, 16)])
            wait("sync", vM, vM.n)
            wait("sync", tM, tM.n)
            op("sync", lambda e: e.dma_start(out=cosb[:, :], in_=cosq_d), [(dC, 16)])
            op("sync", lambda e: e.dma_start(out=sinb[:, :], in_=sinq_d), [(dC, 16)])
            c_msk = op("sync", lambda e: e.dma_start(
                out=maskb[:, :, :], in_=maskf_d.rearrange("a p b -> p a b")), [(dC, 16)])
            c_cos = c_msk

            hb_done = {}
            for qg in range(NQG):
                xq_mm = {}
                xq_done = {}
                for c in range(NC):
                    if c >= 2:
                        wait("vector", tM, xq_mm[c - 2])
                    xq_done[c] = op("vector", lambda e, c=c, qg=qg:
                                    e.tensor_tensor(xnsq[:, c % 2, :],
                                                    xn[:, c, qg * 512:(qg + 1) * 512],
                                                    xn[:, c, qg * 512:(qg + 1) * 512], OP.mult), [(vM, 1)])
                    if c == 0:
                        wait("tensor", vM, psc_free[0])
                    wait("tensor", vM, xq_done[c])
                    xq_mm[c] = mm(psC[0:1, :], onescol[:, :], xnsq[:, c % 2, :],
                                  c == 0, c == NC - 1, [(tM, 1)] if c == NC - 1 else ())
                wait("vector", tM, xq_mm[NC - 1])
                rv = op("vector", lambda e:
                        e.tensor_scalar_mul(rowtmp[0:1, 0, :], psC[0:1, :], 1.0 / D), [(vM, 1)])
                psc_free[0] = rv
                wait("scalar", vM, rv)
                l1 = op("scalar", lambda e:
                        e.activation(rowtmp[0:1, 1, :], rowtmp[0:1, 0, :], AF.Ln, bias=epsb[0:1, 0:1]), [(sM, 1)])
                wait("scalar", sM, l1)
                l2 = op("scalar", lambda e:
                        e.activation(hrow[0:1, :], rowtmp[0:1, 1, :], AF.Exp, scale=-0.5), [(sM, 1)])
                wait("tensor", sM, l2)
                wait("tensor", vM, psc_free[0])
                bc = mm(psC[:, :], onesb[0:1, 0:128], hrow[0:1, :], True, True, [(tP, 1)])
                wait("vector", tP, bc)
                for c in range(NC):
                    op("vector", lambda e, c=c, qg=qg:
                       e.scalar_tensor_tensor(hb[:, c, qg * 512:(qg + 1) * 512],
                                              xn[:, c, qg * 512:(qg + 1) * 512],
                                              anw[:, c:c + 1], psC[:, :], OP.mult, OP.mult),
                       [(vM, 1)])
                psc_free[0] = vM.n
                hb_done[qg] = vM.n
            h_done_v = vM.n
            if DEBUG and s == 0:
                wait("gpsimd", vM, vM.n)
                op("gpsimd", lambda e: e.dma_start(out=dbg_hb, in_=hb[:, :, :]), [(dX, 16)])

            evac3 = {}
            jk3 = 0
            for qg in range(NQG):
                for kc in range(2):
                    sl = psa_slot(jk3)
                    wait("tensor", dW, c_brow)
                    wait("tensor", vM, hb_done[qg])
                    if jk3 >= 4:
                        wait("tensor", vM, evac3[jk3 - 4])
                    pairs = [(wqkvf[:, c, kc * 128:(kc + 1) * 128],
                              hb[:, c, qg * 512:(qg + 1) * 512]) for c in range(NC)]
                    pairs.append((qkvbrow[0:1, 1024 + kc * 128:1024 + (kc + 1) * 128],
                                  onesb[0:1, :]))
                    g = mm_group(sl, pairs, [(tM, 1)])
                    wait("vector", tM, g)
                    evac3[jk3] = op("vector", lambda e, sl=sl, kc=kc, qg=qg:
                                    e.tensor_copy(Kf[:, kc, qg * 512:(qg + 1) * 512], sl), [(vM, 1)])
                    jk3 += 1
            op("vector", lambda e: e.memset(
                Vf[:, :, :].rearrange("p t (h x) -> p t h x", x=65)[:, :, :, 64:65], 1.0), [(vM, 1)])
            jb3 = 0
            evacB3 = {}
            for tt in range(NTT):
                pb = psB[jb3 % 2]
                if jb3 >= 2:
                    wait("tensor", vM, evacB3[jb3 - 2])
                pairs = [(hb[:, c, tt * 128:(tt + 1) * 128], wqkvf[:, c, 256:512])
                         for c in range(NC)]
                pairs.append((onesb[0:1, 0:128], qkvbrow[0:1, 1280:1536]))
                g = mm_group(pb[:, 0:256], pairs, [(tM, 1)])
                wait("vector", tM, g)
                dst = Vf[:, tt, :].rearrange("p (a b) -> p a b", a=KVH, b=65)[:, :, 0:64]
                evacB3[jb3] = op("vector", lambda e, pb=pb, dst=dst:
                                 e.tensor_copy(dst, pb[:, 0:256].rearrange("p (a b) -> p a b", a=KVH, b=64)),
                                 [(vM, 1)])
                jb3 += 1

            wait("vector", dC, c_cos)
            for kc in range(2):
                for colh in range(2):
                    kap = Kf[:, kc, colh * 1024:(colh + 1) * 1024]
                    csl = cosb[:, colh * 1024:(colh + 1) * 1024]
                    snl = sinb[:, colh * 1024:(colh + 1) * 1024]
                    sw0 = swp[:, 0, :]
                    sw1 = swp[:, 1, :]
                    last_cp = 0
                    for blk in range(4):
                        srow = [32, 0, 96, 64][blk]
                        drow = blk * 32
                        last_cp = op("vector", lambda e, kap=kap, srow=srow, drow=drow:
                                     e.tensor_copy(swp[drow:drow + 32, 0, :],
                                                   kap[srow:srow + 32, :]), [(vM, 1)])
                    wait("vector", vM, last_cp)
                    u1 = op("vector", lambda e, kap=kap, csl=csl, sw1=sw1:
                            e.tensor_tensor(sw1, kap, csl, OP.mult), [(vM, 1)])
                    u2 = op("vector", lambda e, snl=snl, sw0=sw0:
                            e.tensor_tensor(sw0, sw0, snl, OP.mult), [(vM, 1)])
                    wait("vector", vM, u2)
                    op("vector", lambda e, kap=kap, sw0=sw0, sw1=sw1:
                       e.tensor_tensor(kap, sw1, sw0, OP.add), [(vM, 1)])
            krope_done = vM.n
            if DEBUG and s == 0:
                wait("gpsimd", vM, vM.n)
                op("gpsimd", lambda e: e.dma_start(out=dbg_kf, in_=Kf[:, :, :]), [(dX, 16)])

            evmulF = {}
            recipF = {}

            def denF_s1(sl_, hname):
                wait("vector", tM, tM.n)
                wait("vector", tP, tP.n)
                d1 = op("vector", lambda e, hname=hname, sl_=sl_:
                        e.tensor_scalar_add(rowtmp[0:1, sl_ % 2, :], psB[sl_ % 2][64:65, :],
                                            sinkexp[0:1, hname:hname + 1]), [(vM, 1)])
                wait("scalar", vM, d1)
                r0 = op("scalar", lambda e, sl_=sl_:
                        e.activation(rowtmp[0:1, sl_ % 2, :], rowtmp[0:1, sl_ % 2, :], AF.Ln), [(sM, 1)])
                wait("scalar", sM, r0)
                recipF[sl_] = op("scalar", lambda e, sl_=sl_:
                                 e.activation(hrow2[0:1, sl_ % 2, :], rowtmp[0:1, sl_ % 2, :],
                                              AF.Exp, scale=-1.0), [(sM, 1)])

            def denF_s2(sl_):
                base, ch = 64 * (sl_ % 2), sl_ // 2
                wait("tensor", sM, recipF[sl_])
                wait("tensor", vM, psc_free[0])
                bc = mm(psC[0:64, :], onesb[0:1, 0:64], hrow2[0:1, sl_ % 2, :], True, True, [(tP, 1)])
                wait("vector", tP, bc)
                d3 = op("vector", lambda e, base=base, ch=ch, sl_=sl_:
                        e.tensor_copy(yf[base:base + 64, ch, :], psB[sl_ % 2][0:64, :]), [(vM, 1)])
                wait("vector", vM, d3)
                evmulF[sl_] = op("vector", lambda e, base=base, ch=ch:
                                 e.tensor_tensor(yf[base:base + 64, ch, :],
                                                 yf[base:base + 64, ch, :],
                                                 psC[0:64, :], OP.mult), [(vM, 1)])
                psc_free[0] = evmulF[sl_]

            gi3 = 0
            fexp = {}
            fmask = {}
            fpv = {}
            first_xxf = [True]
            dma_xxf = {}
            pend3 = []
            psT_free = [p1e[NTT - 1]]
            wait("gpsimd", tM, tM.n)
            c_wfq = op("gpsimd", lambda e:
                       e.dma_start(out=wqkvf[:, :, 0:1024], in_=qkvwf_r[:, :, 0:1024]),
                       [(dW, 16)])
            qrope_q = {}
            jk3_box = [jk3]

            def emit_qblock(qgq, psa_guard):
                jk3q = jk3_box[0]
                for qc in range(NC):
                    sl = psa_slot(jk3q)
                    wait("tensor", dW, c_wfq)
                    if qc == 0 and psa_guard:
                        wait("tensor", sM, psa_guard)
                    if jk3q >= 4:
                        wait("tensor", vM, evac3[jk3q - 4])
                    pairs = [(wqkvf[:, c, qc * 128:(qc + 1) * 128],
                              hb[:, c, qgq * 512:(qgq + 1) * 512]) for c in range(NC)]
                    pairs.append((qkvbrow[0:1, qc * 128:(qc + 1) * 128], onesb[0:1, :]))
                    g = mm_group(sl, pairs, [(tM, 1)])
                    wait("vector", tM, g)
                    evac3[jk3q] = op("vector", lambda e, sl=sl, qc=qc:
                                     e.tensor_copy(Qf[:, qc, :], sl), [(vM, 1)])
                    jk3q += 1
                jk3_box[0] = jk3q
                for qc in range(NC):
                    qap = Qf[:, qc, :]
                    cs = cosb[:, qgq * 512:(qgq + 1) * 512]
                    sn = sinb[:, qgq * 512:(qgq + 1) * 512]
                    sw0 = swp[:, 0, 0:512]
                    sw1 = swp[:, 1, 0:512]
                    last_cp = 0
                    for blk in range(4):
                        srow = [32, 0, 96, 64][blk]
                        drow = blk * 32
                        last_cp = op("vector", lambda e, qap=qap, srow=srow, drow=drow:
                                     e.tensor_copy(swp[drow:drow + 32, 0, 0:512],
                                                   qap[srow:srow + 32, :]), [(vM, 1)])
                    wait("vector", vM, last_cp)
                    u1 = op("vector", lambda e, qap=qap, cs=cs, sw1=sw1:
                            e.tensor_tensor(sw1, qap, cs, OP.mult), [(vM, 1)])
                    u2 = op("vector", lambda e, sn=sn, sw0=sw0:
                            e.tensor_tensor(sw0, sw0, sn, OP.mult), [(vM, 1)])
                    wait("vector", vM, u2)
                    op("vector", lambda e, qap=qap, sw0=sw0, sw1=sw1:
                       e.tensor_tensor(qap, sw1, sw0, OP.add), [(vM, 1)])
                qrope_q[qgq] = vM.n

            emit_qblock(0, 0)
            for qg in range(NQG):
                qrope_done = qrope_q[qg]

                kts = [k for k in range(4 * qg - 1, 4 * qg + 4) if 0 <= k < NTT]
                items3 = [(slp, kt, half)
                          for slp in range(HF // 2) for kt in kts for half in range(2)]
                gi30 = gi3

                def slotA(h):
                    return psA[(h % 4) // 2][:, (h % 2) * 512:(h % 2 + 1) * 512]

                def slotP(h):
                    return Pb3[:, h % 4, :]

                def emit_pv3(idx2):
                    slp2, kt2, half2 = items3[idx2]
                    h2_ = gi30 + idx2
                    wait("tensor", vM, fmask[h2_])
                    if kt2 == kts[0] and slp2 >= 1:
                        slprev = 2 * (slp2 - 1) + half2
                        while pendF and pendF[0][0] <= slprev:
                            denF_s2(pendF.pop(0)[0])
                        wait("tensor", vM, evmulF[slprev])
                    sl2 = 2 * slp2 + half2
                    kv2 = HEAD_ORDER[sl2] // 4
                    fpv[h2_] = mm(psB[half2][0:65, :], Vf[:, kt2, 65 * kv2:65 * kv2 + 65],
                                  slotP(h2_), kt2 == kts[0], kt2 == kts[-1], [(tM, 1)])
                    if kt2 == kts[-1]:
                        denF_s1(sl2, HEAD_ORDER[sl2])
                        pendF.append((sl2, h2_))

                pendF = []
                for idx, (slp, kt, half) in enumerate(items3):
                    j = kt - 4 * qg + 1
                    if idx == 0:
                        wait("tensor", vM, qrope_done)
                        wait("tensor", vM, krope_done)
                        wait("tensor", dC, c_msk)
                    if gi3 >= 4:
                        wait("tensor", sM, fexp[gi3 - 4])
                    sl_ = 2 * slp + half
                    kv = HEAD_ORDER[sl_] // 4
                    kch, kbase = kv // 2, 64 * (kv % 2)
                    stm = mm(slotA(gi3),
                             Kf[kbase:kbase + 64, kch, kt * 128:(kt + 1) * 128],
                             Qf[64 * half:64 * half + 64, slp, :], True, True,
                             [(tM, 1)])
                    if pendF and gi3 - pendF[0][1] >= 4:
                        denF_s2(pendF.pop(0)[0])
                    if pend3 and idx % 8 == 7:
                        pend3.pop(0)()
                    if idx >= 1:
                        emit_pv3(idx - 1)
                    wait("scalar", tM, stm)
                    if gi3 >= 4:
                        wait("scalar", tM, fpv[gi3 - 4])
                    fexp[gi3] = op("scalar", lambda e, gi3=gi3:
                                   e.activation(slotP(gi3), slotA(gi3),
                                                AF.Exp, scale=1.0 / 8.0), [(sM, 1)])
                    wait("vector", sM, fexp[gi3])
                    fmask[gi3] = op("vector", lambda e, gi3=gi3, j=j:
                                    e.tensor_tensor(slotP(gi3), slotP(gi3),
                                                    maskb[:, j, :], OP.mult), [(vM, 1)])
                    gi3 += 1
                emit_pv3(len(items3) - 1)
                while pendF:
                    denF_s2(pendF.pop(0)[0])

                if DEBUG and s == 0 and qg == 0:
                    wait("gpsimd", vM, vM.n)
                    op("gpsimd", lambda e: e.dma_start(out=dbg_yf, in_=yf[:, :, :]), [(dX, 16)])
                if qg < NQG - 1:
                    emit_qblock(qg + 1, fexp[gi3 - 1])
                xf_v = 0
                for ofc in range(NC):
                    jk3p = jk3_box[0]
                    sl = psa_slot(jk3p)
                    wait("tensor", dW, c_wproj)
                    if qg == NQG - 1 and ofc == 0:
                        wait("tensor", sM, fexp[gi3 - 1])
                    if jk3p >= 4:
                        wait("tensor", vM, evac3[jk3p - 4])
                    pairs = [(wproj[:, c, ofc * 128:(ofc + 1) * 128], yf[:, c, :]) for c in range(NC)]
                    pairs.append((projbrow[0:1, ofc * 128:(ofc + 1) * 128], onesb[0:1, :]))
                    g = 0
                    for ci, (lf, rf) in enumerate(pairs):
                        if ofc == 0 and ci < NC:
                            wait("tensor", vM, evmulF[2 * ci + 1])
                        g = mm(sl, lf, rf, ci == 0, ci == len(pairs) - 1,
                               [(tM, 1)] if ci == len(pairs) - 1 else ())
                    wait("vector", tM, g)
                    evac3[jk3p] = op("vector", lambda e, sl=sl, ofc=ofc, qg=qg:
                                     e.tensor_tensor(xn[:, ofc, qg * 512:(qg + 1) * 512],
                                                     xn[:, ofc, qg * 512:(qg + 1) * 512],
                                                     sl, OP.add), [(vM, 1)])
                    xf_v = evac3[jk3p]
                    jk3_box[0] = jk3p + 1
                xf_done = vM.n

                def mk_rstdh2(qg=qg, xf_done=xf_done):
                    def emit():
                        fq_mm = {}
                        fq = {}
                        for c in range(NC + 1):
                            if c < NC:
                                if c == 0:
                                    wait("vector", vM, xf_done)
                                if c >= 2:
                                    wait("vector", tM, fq_mm.get(c - 2, 0))
                                fq[c] = op("vector", lambda e, c=c, qg=qg:
                                           e.tensor_tensor(xfsq[:, c % 2, :],
                                                           xn[:, c, qg * 512:(qg + 1) * 512],
                                                           xn[:, c, qg * 512:(qg + 1) * 512],
                                                           OP.mult), [(vM, 1)])
                            if c >= 1:
                                cc = c - 1
                                if cc == 0:
                                    wait("tensor", vM, psc_free[0])
                                wait("tensor", vM, fq[cc])
                                fq_mm[cc] = mm(psC[0:1, :], onescol[:, :], xfsq[:, cc % 2, :],
                                               cc == 0, cc == NC - 1,
                                               [(tM, 1)] if cc == NC - 1 else ())
                        wait("vector", tM, fq_mm[NC - 1])
                        rv2 = op("vector", lambda e:
                                 e.tensor_scalar_mul(rstrow[0:1, 0, :], psC[0:1, :], 1.0 / D), [(vM, 1)])
                        psc_free[0] = rv2
                        wait("scalar", vM, rv2)
                        m1 = op("scalar", lambda e:
                                e.activation(rstrow[0:1, 1, :], rstrow[0:1, 0, :], AF.Ln,
                                             bias=epsb[0:1, 0:1]), [(sM, 1)])
                        wait("scalar", sM, m1)
                        op("scalar", lambda e, qg=qg:
                           e.activation(rstdh2[0:1, qg * 512:(qg + 1) * 512], rstrow[0:1, 1, :],
                                        AF.Exp, scale=-0.5), [(sM, 1)])
                    return emit

                def mk_assembly(t, xf_done=xf_done):
                    def emit():
                        if first_xxf[0]:
                            first_xxf[0] = False
                            wait("gpsimd", dOA[0], a_writes_done[0])
                            wait("gpsimd", dOA[1], a_writes_done[1])
                        wait("sync", vM, vM.n)
                        wait("sync", dX3, dX3.n)
                        ldx = op("sync", lambda e, s=s, t=t:
                                 e.dma_start(out=xbuf3[:, :], in_=xr[:, s, t, :]), [(dX3, 16)])
                        wait("tensor", vM, xf_done)
                        wait("tensor", vM, psT_free[0])
                        trx = 0
                        for c in range(NC):
                            trx = op("tensor", lambda e, c=c, t=t:
                                     e.transpose(psT[:, c * 128:(c + 1) * 128],
                                                 xn[:, c, t * 128:(t + 1) * 128], ident[:, :]),
                                     [(tM, 1)] if c == NC - 1 else ())
                        wait("vector", tM, trx)
                        wait("vector", dX3, ldx)
                        if t >= 2:
                            wait("vector", dOX[t % 2], dma_xxf[t - 2])
                        vv = op("vector", lambda e, t=t:
                                e.tensor_tensor(tmp3[:, t % 2, :], xbuf3[:, :],
                                                psT[:, :], OP.add), [(vM, 1)])
                        psT_free[0] = vv
                        wait("gpsimd", vM, vv)
                        dma_xxf[t] = op("gpsimd", lambda e, s=s, t=t:
                                        e.dma_start(out=outr[:, s, t, :], in_=tmp3[:, t % 2, :],
                                                    accum_op=OP.add), [(dOX[t % 2], 16)])
                    return emit

                pend3.append(mk_rstdh2())
                for tt4 in range(4):
                    pend3.append(mk_assembly(qg * 4 + tt4))
            while pend3:
                pend3.pop(0)()
            xxf_done = (dOX[0].n, dOX[1].n)
            if DEBUG and s == 0:
                wait("gpsimd", vM, vM.n)
                wait("gpsimd", sM, sM.n)
                op("gpsimd", lambda e: e.dma_start(out=dbg_xf, in_=xn[:, :, :]), [(dX, 16)])
                op("gpsimd", lambda e: e.dma_start(out=dbg_rstdh2, in_=rstdh2[0:1, :]), [(dX, 16)])

            wait("gpsimd", tM, tM.n)
            wait("gpsimd", vM, vM.n)
            wait("gpsimd", sM, sM.n)
            for qg in range(NQG):
                wait("tensor", vM, psc_free[0])
                wait("tensor", sM, sM.n)
                bc = mm(psC[:, :], onesb[0:1, 0:128], rstdh2[0:1, qg * 512:(qg + 1) * 512],
                        True, True, [(tP, 1)])
                wait("vector", tP, bc)
                for c in range(NC):
                    op("vector", lambda e, c=c, qg=qg:
                       e.scalar_tensor_tensor(h2[:, c, qg * 512:(qg + 1) * 512],
                                              xn[:, c, qg * 512:(qg + 1) * 512],
                                              fnw[:, c:c + 1], psC[:, :], OP.mult, OP.mult),
                       [(vM, 1)])
                psc_free[0] = vM.n
            h2_done = vM.n

            gcopy = {}
            for t in range(NTT):
                pb = psB[t % 2]
                wait("tensor", vM, h2_done)
                if t >= 2:
                    wait("tensor", vM, gcopy[t - 2])
                pairs = [(h2[:, c, t * 128:(t + 1) * 128], gatew[:, c, :]) for c in range(NC)]
                pairs.append((onesb[0:1, 0:128], gateb[0:1, :]))
                g = mm_group(pb[:, 0:NE], pairs, [(tM, 1)])
                wait("vector", tM, g)
                gcopy[t] = op("vector", lambda e, pb=pb, t=t:
                              e.tensor_copy(gwork[:, t, :], pb[:, 0:NE]), [(vM, 1)])
            rm = {}; req = {}; rmask = {}; rexp = {}; rsum = {}; rrec = {}; rsc = {}; rcomb = {}
            for r in range(4):
                for t in range(NTT):
                    if r == 0:
                        wait("vector", vM, gcopy[t])
                    else:
                        wait("vector", vM, rmask[(r - 1, t)])
                    rm[(r, t)] = op("vector", lambda e, t=t, r=r:
                                    e.reduce_max(gm[:, t, r:r + 1], gwork[:, t, :], axis=AX.X), [(vM, 1)])
                for t in range(NTT):
                    wait("vector", vM, rm[(r, t)])
                    req[(r, t)] = op("vector", lambda e, t=t, r=r:
                                     e.tensor_scalar(geq[:, t, r, :], gwork[:, t, :], gm[:, t, r:r + 1],
                                                     None, OP.is_equal), [(vM, 1)])
                if r < 3:
                    for t in range(NTT):
                        wait("vector", vM, req[(r, t)])
                        rmask[(r, t)] = op("vector", lambda e, t=t, r=r:
                                           e.scalar_tensor_tensor(gwork[:, t, :], geq[:, t, r, :], -1e30,
                                                                  gwork[:, t, :], OP.mult, OP.add), [(vM, 1)])
            for t in range(NTT):
                wait("scalar", vM, req[(3, t)])
                rexp[t] = op("scalar", lambda e, t=t:
                             e.activation(gm[:, t, 4:8], gm[:, t, 0:4], AF.Exp), [(sM, 1)])
            for t in range(NTT):
                wait("vector", sM, rexp[t])
                rsum[t] = op("vector", lambda e, t=t:
                             e.reduce_sum(gm[:, t, 3:4], gm[:, t, 4:8], axis=AX.X), [(vM, 1)])
            for t in range(NTT):
                wait("vector", vM, rsum[t])
                rrec[t] = op("vector", lambda e, t=t:
                             e.reciprocal(gm[:, t, 2:3], gm[:, t, 3:4]), [(vM, 1)])
            for t in range(NTT):
                wait("vector", vM, rrec[t])
                rsc[t] = op("vector", lambda e, t=t:
                            e.tensor_scalar_mul(gm[:, t, 4:8], gm[:, t, 4:8], gm[:, t, 2:3]), [(vM, 1)])
            for t in range(NTT):
                wait("vector", vM, rsc[t])
                op("vector", lambda e, t=t, s=s:
                   e.tensor_copy(gwsave[:, s, t, :], gm[:, t, 4:8]), [(vM, 1)])
            gv = gwork[:, :, :]
            w1 = op("vector", lambda e: e.tensor_tensor(gv, geq[:, :, 0, :], geq[:, :, 1, :], OP.add), [(vM, 1)])
            wait("vector", vM, w1)
            w2 = op("vector", lambda e: e.tensor_tensor(gv, gv, geq[:, :, 2, :], OP.add), [(vM, 1)])
            wait("vector", vM, w2)
            w3 = op("vector", lambda e: e.tensor_tensor(gv, gv, geq[:, :, 3, :], OP.add), [(vM, 1)])
            wait("vector", vM, w3)
            cind = op("vector", lambda e: e.tensor_copy(indb[:, :, :], gv), [(vM, 1)])
            wait("tensor", vM, psc_free[0])
            wait("tensor", vM, cind)
            wait("tensor", dC, c_lt)
            cmm = 0
            for t in range(NTT):
                cmm = mm(psC[0:32, t:t + 1], indb[:, t, :], onescol[:, :], True, True,
                         [(tM, 1)] if t == NTT - 1 else ())
            wait("vector", tM, cmm)
            cevac = op("vector", lambda e: e.tensor_copy(cntsb[0:32, 0:16], psC[0:32, 0:16]), [(vM, 1)])
            psc_free[0] = cevac
            wait("gpsimd", vM, cevac)
            cd1 = op("gpsimd", lambda e: e.dma_start(out=cnt_d, in_=cntsb[0:32, 0:16]), [(dW, 16)])
            wait("gpsimd", dW, cd1)
            cd2 = op("gpsimd", lambda e: e.dma_start(
                out=cntC[0:16, 0:32], in_=cnt_d.rearrange("e t -> t e"),
                allow_slow_non_contiguous=True), [(dW, 16)])
            wait("tensor", dW, cd2)
            mm(psB[0][0:16, 0:32], lt16b[:, :], cntC[0:16, 0:32], True, False)
            bmm = mm(psB[0][0:16, 0:32], onesb[0:1, 0:16], eoffb[0:1, 32 * s:32 * s + 32],
                     False, True, [(tM, 1)])
            wait("vector", tM, bmm)
            bevac = op("vector", lambda e: e.tensor_copy(basesb[0:16, 0:32], psB[0][0:16, 0:32]), [(vM, 1)])
            wait("gpsimd", vM, bevac)
            bd1 = op("gpsimd", lambda e: e.dma_start(out=base_d, in_=basesb[0:16, 0:32]), [(dW, 16)])
            wait("gpsimd", dW, bd1)
            bd2 = op("gpsimd", lambda e: e.dma_start(
                out=baserow[0:1, :], in_=base_d.rearrange("t e -> (t e)").unsqueeze(0)), [(dW, 16)])
            wait("tensor", dW, bd2)
            wait("tensor", vM, psc_free[0])
            bfmm = mm(psC[:, :], onesf[0:1, :], baserow[0:1, :], True, True, [(tP, 1)])
            wait("vector", tP, bfmm)
            bfev = op("vector", lambda e: e.tensor_copy(bfield[:, :], psC[:, :]), [(vM, 1)])
            psc_free[0] = bfev
            if s == SPC - 1:
                pf_wug = op("gpsimd", lambda e:
                            e.dma_start(out=wu[:, 0, :, :], in_=wup_r[:, 0, :, 1024:2048]), [(dWug, 16)])
                pf_wuu = op("gpsimd", lambda e:
                            e.dma_start(out=wu[:, 1, :, :], in_=wup_r[:, 0, :, 0:1024]), [(dWuu, 16)])
                pf_wd = op("gpsimd", lambda e:
                           e.dma_start(out=wd[:, :, :], in_=wdn_r[:, 0, :, :]), [(dWdd, 16)])
            sc_done = {}
            sf_done = {}
            trh_done = {}
            ev_done = {}
            for t in range(NTT):
                pbr = psB[t % 2]
                if t >= 2:
                    wait("tensor", vM, sf_done[t - 2])
                else:
                    wait("tensor", vM, bfev)
                rmm = mm(pbr[:, 0:NE], lt128b[:, :], indb[:, t, :], True, True, [(tM, 1)])
                wait("vector", tM, rmm)
                wait("vector", dSC[t % 2], sc_done.get(t - 2, 0))
                sf0 = op("vector", lambda e, pbr=pbr, t=t:
                         e.tensor_tensor(sfield[:, :], pbr[:, 0:NE], bfield[:, 32 * t:32 * t + 32],
                                         OP.add), [(vM, 1)])
                sf_done[t] = sf0
                last = sf0
                for r in range(4):
                    wait("vector", vM, last)
                    m1_ = op("vector", lambda e, t=t, r=r:
                             e.tensor_tensor(gwork[:, t, :], geq[:, t, r, :], sfield[:, :], OP.mult),
                             [(vM, 1)])
                    wait("vector", vM, m1_)
                    last = op("vector", lambda e, t=t, r=r:
                              e.reduce_sum(sfl[:, t, r:r + 1], gwork[:, t, :], axis=AX.X), [(vM, 1)])
                wait("vector", vM, last)
                ci = op("vector", lambda e, t=t, s=s:
                        e.tensor_copy(selint[:, s, t, :], sfl[:, t, :]), [(vM, 1)])
                wait("tensor", vM, h2_done)
                if t >= 1:
                    wait("tensor", vM, ev_done[t - 1])
                trx = 0
                for c in range(NC):
                    trx = op("tensor", lambda e, c=c, t=t:
                             e.transpose(psT[:, c * 128:(c + 1) * 128],
                                         h2[:, c, t * 128:(t + 1) * 128], ident[:, :]),
                             [(tM, 1)] if c == NC - 1 else ())
                trh_done[t] = trx
                wait("vector", tM, trx)
                wait("vector", dSC[t % 2], sc_done.get(t - 2, 0))
                ev_done[t] = op("vector", lambda e, t=t:
                                e.tensor_copy(h2tm[:, t % 2, :], psT[:, :]), [(vM, 1)])
                wait("gpsimd", vM, ev_done[t])
                wait("gpsimd", vM, ci)
                last_sc = 0
                for r in range(4):
                    last_sc = op("gpsimd", lambda e, t=t, r=r, s=s:
                                 e.indirect_dma_start(
                                     out=gbuf_d,
                                     out_offset=IndirectOffsetOnAxis(selint[:, s, t, r:r + 1], 0),
                                     in_=h2tm[:, t % 2, :], in_offset=None), [(dSC[t % 2], 16)])
                sc_done[t] = last_sc
            seq_scatter_done[s] = (dSC[0].n, dSC[1].n)
            if DEBUG and s == 0:
                wait("gpsimd", vM, vM.n)
                op("gpsimd", lambda e: e.dma_start(out=dbg_h2, in_=h2[:, :, :]), [(dX, 16)])
                op("gpsimd", lambda e: e.dma_start(out=dbg_base, in_=baserow[0:1, :]), [(dX, 16)])
                op("gpsimd", lambda e: e.dma_start(out=dbg_sel[:, 0, :, :], in_=selint[:, 0, :, :]), [(dX, 16)])
                op("gpsimd", lambda e: e.dma_start(out=dbg_gw[:, 0, :, :], in_=gwsave[:, 0, :, :]), [(dX, 16)])
                op("gpsimd", lambda e: e.dma_start(out=dbg_geq, in_=geq[:, :, :, :]), [(dX, 16)])
                op("gpsimd", lambda e: e.dma_start(out=dbg_bf, in_=bfield[:, :]), [(dX, 16)])
                op("gpsimd", lambda e: e.dma_start(out=dbg_sfl, in_=sfl[:, :, :]), [(dX, 16)])
                wait("gpsimd", dW, dW.n)
                op("gpsimd", lambda e: e.dma_start(out=dbg_gbuf, in_=gbuf_d[0:256, :]), [(dX, 16)])

        wait("gpsimd", tM, tM.n)
        wait("gpsimd", vM, vM.n)
        wait("gpsimd", sM, sM.n)
        wug_loaded = pf_wug
        wuu_loaded = pf_wuu
        wd_loaded = pf_wd
        wait("sync", tM, tM.n)
        wait("sync", vM, vM.n)
        wait("sync", dSC[0], dSC[0].n)
        wait("sync", dSC[1], dSC[1].n)
        gtm_loaded = op("sync", lambda e:
                        e.dma_start(out=Gtm[:, 0, :, :],
                                    in_=gbuf_d[0:1024, :].rearrange("(t p) f -> p t f", p=128)),
                        [(dWgt, 16)])
        prev_umul_last = 0
        prev_down_t = 0

        def gtr_tt(e_, tt, load_cnt, st):
            gfm = G_fm1 if e_ % 2 == 0 else G_fm2
            if tt == 0:
                wait("tensor", dWgt, load_cnt)
            if st["prev"]:
                wait("tensor", vM, st["prev"])
            trx = 0
            for c in range(NC):
                trx = op("tensor", lambda e, c=c, tt=tt, e_=e_:
                         e.transpose(psT[:, c * 128:(c + 1) * 128],
                                     Gtm[:, e_ % 2, tt, c * 128:(c + 1) * 128], ident[:, :]),
                         [(tM, 1)] if c == NC - 1 else ())
            wait("vector", tM, trx)
            st["prev"] = op("vector", lambda e, tt=tt, gfm=gfm:
                            e.tensor_copy(gfm[:, :, tt * 128:(tt + 1) * 128],
                                          psT[:, :].rearrange("p (a b) -> p a b", a=NC, b=128)),
                            [(vM, 1)])

        def emit_gtr(e_, load_cnt, prev_gfm_use_t):
            st = {"prev": gfm_freed.get(e_ % 2, 0)}
            for tt in range(8):
                gtr_tt(e_, tt, load_cnt, st)
            return st["prev"], tM.n

        gfm_freed = {}
        gfm_ready = {}
        gfm_freed[0] = vM.n
        gfm_ready[0], gtr_t = emit_gtr(0, gtm_loaded, 0)
        wait("sync", tM, gtr_t)
        gtm_loaded = op("sync", lambda e:
                        e.dma_start(out=Gtm[:, 1, :, :],
                                    in_=gbuf_d[1024:2048, :]
                                    .rearrange("(t p) f -> p t f", p=128)), [(dWgt, 16)])
        for e_ in range(NE):
            G_fm = G_fm1 if e_ % 2 == 0 else G_fm2
            silu_done = {}
            for jj in range(8):
                tg, fp = jj // 4, jj % 4
                if jj == 0:
                    wait("tensor", dWug, wug_loaded)
                    wait("tensor", vM, gfm_ready[e_])
                    if e_ > 0:
                        wait("tensor", vM, prev_dmul_last)
                if jj >= 2:
                    wait("tensor", sM, silu_done[jj - 2])
                gph = 0
                for half in range(2):
                    ffc = 2 * fp + half
                    gph = mm_group(psA[jj % 2][:, half * 512:(half + 1) * 512],
                                   [(wu[:, 0, c, ffc * 128:(ffc + 1) * 128],
                                     G_fm[:, c, tg * 512:(tg + 1) * 512]) for c in range(NC)],
                                   [(tM, 1)] if half == 1 else ())
                wait("scalar", tM, gph)
                if jj == 0 and e_ > 0:
                    wait("scalar", tM, prev_down_t)
                silu_done[jj] = op("scalar", lambda e, jj=jj, fp=fp, tg=tg:
                                   e.activation(act[:, 2 * fp:2 * fp + 2, tg * 512:(tg + 1) * 512],
                                                psA[jj % 2][:, :], AF.Silu), [(sM, 1)])
            gphase_last = tM.n
            wait("gpsimd", tM, gphase_last)
            if e_ < NE - 1:
                wait("gpsimd", dWug, dWug.n)
                wug_loaded = op("gpsimd", lambda e, e_=e_:
                                e.dma_start(out=wu[:, 0, :, :],
                                            in_=wup_r[:, e_ + 1, :, 1024:2048]), [(dWug, 16)])
            umul = {}
            for jj in range(16):
                tg, ffc = jj // 8, jj % 8
                if jj == 0:
                    wait("tensor", dWuu, wuu_loaded)
                    if e_ > 0:
                        wait("tensor", vM, prev_umul_last)
                if jj >= 2:
                    wait("tensor", vM, umul[jj - 2])
                g = mm_group(psB[jj % 2][:, :],
                             [(wu[:, 1, c, ffc * 128:(ffc + 1) * 128],
                               G_fm[:, c, tg * 512:(tg + 1) * 512]) for c in range(NC)],
                             [(tM, 1)])
                wait("vector", tM, g)
                wait("vector", sM, silu_done[(tg * 4) + ffc // 2])
                umul[jj] = op("vector", lambda e, jj=jj, ffc=ffc, tg=tg:
                              e.tensor_tensor(act[:, ffc, tg * 512:(tg + 1) * 512],
                                              psB[jj % 2][:, :],
                                              act[:, ffc, tg * 512:(tg + 1) * 512],
                                              OP.mult), [(vM, 1)])
            uphase_last = tM.n
            prev_umul_last = umul[15]
            gtr_pend = []
            if e_ < NE - 1:
                gst = {"prev": gfm_freed.get((e_ + 1) % 2, 0)}
                gfm_freed[(e_ + 1) % 2] = prev_umul_last if e_ >= 1 else 0
                gst["prev"] = gfm_freed[(e_ + 1) % 2]
                for tt in range(8):
                    gtr_pend.append(lambda tt=tt, gst=gst, e2=e_ + 1, lc=gtm_loaded:
                                    gtr_tt(e2, tt, lc, gst))
                def fin_gtr(gst=gst, e2=e_ + 1):
                    gfm_ready[e2] = gst["prev"]
                gtr_pend.append(fin_gtr)
                if e_ < NE - 2:
                    wait("sync", tM, gphase_last)
                    wait("sync", dWgt, dWgt.n)
                    gtm_loaded = op("sync", lambda e, e_=e_:
                                    e.dma_start(out=Gtm[:, e_ % 2, :, :],
                                                in_=gbuf_d[(e_ + 2) * 1024:(e_ + 3) * 1024, :]
                                                .rearrange("(t p) f -> p t f", p=128)), [(dWgt, 16)])
            wait("gpsimd", tM, uphase_last)
            if e_ < NE - 1:
                wait("gpsimd", dWuu, dWuu.n)
                wuu_loaded = op("gpsimd", lambda e, e_=e_:
                                e.dma_start(out=wu[:, 1, :, :],
                                            in_=wup_r[:, e_ + 1, :, 0:1024]), [(dWuu, 16)])
            dmul = {}
            ddma = {}
            for jj in range(16):
                tt, og = jj // 2, jj % 2
                if jj == 0:
                    wait("tensor", dWdd, wd_loaded)
                    wait("tensor", vM, umul[15])
                    wait("tensor", sM, silu_done[7])
                if jj >= 4:
                    wait("tensor", vM, dmul[jj - 4])
                g = mm_group(psa_slot(jj)[:, :] if False else psa_slot(jj),
                             [(act[:, c, tt * 128:(tt + 1) * 128],
                               wd[:, c, og * 512:(og + 1) * 512]) for c in range(NC)],
                             [(tM, 1)])
                if gtr_pend and jj % 2 == 1:
                    gtr_pend.pop(0)()
                wait("vector", tM, g)
                if jj >= 2:
                    wait("vector", dDN[jj % 2], ddma[jj - 2])
                dmul[jj] = op("vector", lambda e, jj=jj:
                              e.tensor_copy(tmpbf[:, jj % 2, :], psa_slot(jj)), [(vM, 1)])
                wait("scalar", vM, dmul[jj])
                ddma[jj] = op("scalar", lambda e, e_=e_, tt=tt, og=og, jj=jj:
                              e.dma_start(out=dnbuf_d[e_ * 1024 + tt * 128:e_ * 1024 + (tt + 1) * 128,
                                                      og * 512:(og + 1) * 512],
                                          in_=tmpbf[:, jj % 2, :]), [(dDN[jj % 2], 16)])
            while gtr_pend:
                gtr_pend.pop(0)()
            prev_down_t = tM.n
            prev_dmul_last = vM.n
            if e_ < NE - 1:
                wait("gpsimd", tM, prev_down_t)
                wait("gpsimd", dWdd, dWdd.n)
                wd_loaded = op("gpsimd", lambda e, e_=e_:
                               e.dma_start(out=wd[:, :, :], in_=wdn_r[:, e_ + 1, :, :]), [(dWdd, 16)])
        dnbuf_all = (dDN[0].n, dDN[1].n)

        wait("gpsimd", dDN[0], dnbuf_all[0])
        wait("gpsimd", dDN[1], dnbuf_all[1])
        wait("sync", tM, tM.n)
        for eng in ("sync", "gpsimd"):
            wait(eng, dOA[0], dOA[0].n)
            wait(eng, dOA[1], dOA[1].n)
            wait(eng, dOX[0], dOX[0].n)
            wait(eng, dOX[1], dOX[1].n)
        gdma = {}
        gsum = {}
        gld1 = {}
        out_wr = {}
        ldo = {}
        for s in range(SPC):
            for t in range(NTT):
                it = s * NTT + t
                if it >= 2:
                    wait("sync", vM, gld1[it - 2])
                ldo[it] = op("sync", lambda e, s=s, t=t, it=it:
                             e.dma_start(out=tmpo[:, it % 2, :], in_=outr[:, s, t, :]),
                             [(dOL[it % 2], 16)])
                if it >= 2:
                    wait("gpsimd", vM, gsum[it - 2])
                last_g = 0
                for r in range(4):
                    last_g = op("gpsimd", lambda e, s=s, t=t, r=r, it=it:
                                e.indirect_dma_start(
                                    out=gth[:, it % 2, r, :], out_offset=None, in_=dnbuf_d,
                                    in_offset=IndirectOffsetOnAxis(selint[:, s, t, r:r + 1], 0)),
                                [(dGT[it % 2], 16)])
                gdma[it] = last_g
                wait("vector", dGT[it % 2], gdma[it])
                wait("vector", dOL[it % 2], ldo[it])
                if it >= 2:
                    wait("vector", dOG[it % 2], out_wr[it - 2])
                v0 = op("vector", lambda e, s=s, t=t, it=it:
                        e.scalar_tensor_tensor(tmpgs[:, it % 2, :], gth[:, it % 2, 0, :],
                                               gwsave[:, s, t, 0:1],
                                               tmpo[:, it % 2, :], OP.mult, OP.add), [(vM, 1)])
                gld1[it] = v0
                for r in range(1, 4):
                    wait("vector", vM, v0)
                    v0 = op("vector", lambda e, s=s, t=t, r=r, it=it:
                            e.scalar_tensor_tensor(tmpgs[:, it % 2, :], gth[:, it % 2, r, :],
                                                   gwsave[:, s, t, r:r + 1],
                                                   tmpgs[:, it % 2, :], OP.mult, OP.add), [(vM, 1)])
                gsum[it] = v0
                wait("scalar", vM, gsum[it])
                out_wr[it] = op("scalar", lambda e, s=s, t=t, it=it:
                                e.dma_start(out=outr[:, s, t, :], in_=tmpgs[:, it % 2, :]),
                                [(dOG[it % 2], 16)])

        wait("gpsimd", dOG[0], dOG[0].n)
        wait("gpsimd", dOG[1], dOG[1].n)
        wait("gpsimd", dOX[0], dOX[0].n)
        wait("gpsimd", dOX[1], dOX[1].n)
        wait("gpsimd", dOA[0], dOA[0].n)
        wait("gpsimd", dOA[1], dOA[1].n)


        with nc.Block() as block:
            @block.sync
            def _(e):
                for fn in P.q["sync"]:
                    fn(e)

            @block.tensor
            def _(e):
                for fn in P.q["tensor"]:
                    fn(e)

            @block.vector
            def _(e):
                for fn in P.q["vector"]:
                    fn(e)

            @block.scalar
            def _(e):
                for fn in P.q["scalar"]:
                    fn(e)

            @block.gpsimd
            def _(e):
                for fn in P.q["gpsimd"]:
                    fn(e)

    return nc



def _rope_tables():
    pos = np.arange(T, dtype=np.float64) / 32.0
    inv = 1.0 / (150000.0 ** (np.arange(0, DH, 2, dtype=np.float64) / DH))
    fr = pos[None, :] * inv[:, None]
    c = np.cos(fr)
    sn = np.sin(fr)
    cosq = np.tile(c, (4, 1))
    sinq = np.concatenate([-sn, sn, -sn, sn], axis=0)
    return (cosq.astype(np.float32).astype(ml_dtypes.bfloat16),
            sinq.astype(np.float32).astype(ml_dtypes.bfloat16))


def _masks():
    ki = np.arange(128)[:, None]
    qi = np.arange(512)[None, :]
    m = []
    for j in (-1, 0, 1, 2, 3):
        d = qi - ki - 128 * j
        m.append(((d >= 0) & (d <= 128)).astype(np.float32))
    return np.stack(m, 0).astype(ml_dtypes.bfloat16)


def _perm_evod():
    return np.concatenate([np.arange(0, DH, 2), np.arange(1, DH, 2)])


def kernel(**inputs):
    x = np.ascontiguousarray(np.asarray(inputs["L0_attn_x"], dtype=np.float32))
    wqkva = np.ascontiguousarray(np.asarray(inputs["L0_attn_w_qkv"], dtype=np.float32))
    wouta = np.ascontiguousarray(np.asarray(inputs["L0_attn_w_out"], dtype=np.float32))
    anw = np.ascontiguousarray(np.asarray(inputs["L0_ffn_attn_norm_w"], dtype=np.float32))
    fnw = np.ascontiguousarray(np.asarray(inputs["L0_ffn_ffn_norm_w"], dtype=np.float32))
    qkvw = np.asarray(inputs["L0_ffn_qkv_w"], dtype=np.float32)
    qkvb = np.asarray(inputs["L0_ffn_qkv_b"], dtype=np.float32)
    projw = np.asarray(inputs["L0_ffn_proj_w"], dtype=np.float32)
    projb = np.ascontiguousarray(np.asarray(inputs["L0_ffn_proj_b"], dtype=np.float32)[None, :])
    sinkv = np.ascontiguousarray(np.asarray(inputs["L0_ffn_attn_sink"], dtype=np.float32)[None, :])
    gatew = np.ascontiguousarray(np.asarray(inputs["L0_ffn_gate_w"], dtype=np.float32))
    gateb = np.ascontiguousarray(np.asarray(inputs["L0_ffn_gate_b"], dtype=np.float32)[None, :])
    wup = np.ascontiguousarray(np.asarray(inputs["L0_ffn_W_up"], dtype=np.float32))
    wdn = np.ascontiguousarray(np.asarray(inputs["L0_ffn_W_down"], dtype=np.float32))

    ev = _perm_evod()
    qperm = np.concatenate([HEAD_ORDER[i] * DH + ev for i in range(HF)])
    kperm = np.concatenate([1024 + kv * DH + ev for kv in range(KVH)])
    vperm = np.arange(1280, 1536)
    colperm = np.concatenate([qperm, kperm, vperm])
    qkvw_p = np.ascontiguousarray(qkvw[:, colperm])
    qkvb_p = np.ascontiguousarray(qkvb[colperm][None, :])
    rowperm = np.concatenate([HEAD_ORDER[i] * DH + np.arange(DH) for i in range(HF)])
    projw_p = np.ascontiguousarray(projw[rowperm, :])

    ident = np.eye(128, dtype=np.float32).astype(ml_dtypes.bfloat16)
    cosq, sinq = _rope_tables()
    maskf = _masks()
    lt16 = np.triu(np.ones((16, 16), np.float32), 1).astype(ml_dtypes.bfloat16)
    lt128 = np.triu(np.ones((128, 128), np.float32), 1).astype(ml_dtypes.bfloat16)
    eoff = np.zeros((1, 64), np.float32)
    for s_ in range(SPC):
        eoff[0, 32 * s_:32 * s_ + 32] = np.arange(NE) * 1024 + 512 * s_
    eoff = eoff.astype(ml_dtypes.bfloat16)

    common = dict(wqkva=wqkva, wouta=wouta, anw=anw, fnw=fnw,
                  qkvwf=qkvw_p, qkvbf=qkvb_p, projw=projw_p, projb=projb,
                  sink=sinkv, gatew=gatew, gateb=gateb, wup=wup, wdn=wdn,
                  ident=ident, cosq=np.ascontiguousarray(cosq),
                  sinq=np.ascontiguousarray(sinq),
                  maskf=np.ascontiguousarray(maskf),
                  lt16=np.ascontiguousarray(lt16), lt128=np.ascontiguousarray(lt128),
                  eoff=np.ascontiguousarray(eoff))
    in_maps = [dict(common, x=np.ascontiguousarray(x[2 * c:2 * c + 2])) for c in range(NCORES)]

    ncprog = build_program()
    res = run_bass_kernel_spmd(ncprog, in_maps, core_ids=list(range(NCORES)))
    out = np.concatenate([np.asarray(res.results[c]["out"]) for c in range(NCORES)], axis=0)
    return out.astype(np.float32)
